# version 15
# baseline (speedup 1.0000x reference)
import numpy as np
import ml_dtypes
from contextlib import ExitStack
import concourse.bass as bass
import concourse.mybir as mybir
from concourse.bass_utils import run_bass_kernel_spmd

F32 = mybir.dt.float32
BF16 = mybir.dt.bfloat16
AF = mybir.ActivationFunctionType
ALU = mybir.AluOpType
AX = mybir.AxisListType
BFD = ml_dtypes.bfloat16

DM = 2048
DIN = 8512
DI = 4096
DS = 128
NH = 64
DH = 64
CD = 4352
NFT = 34
Q = 128
TT = 512
NSUB = TT // 128
NCH = TT // Q
SEQ = 2048
NSEQ = 2
NTT_PER_SEQ = SEQ // TT
EPS = 1e-5

ENGS = ("tensor", "vector", "scalar", "gpsimd", "sync")


class Prog:

    DMA_WINDOW = 8

    def __init__(self):
        self.sched = {e: [] for e in ENGS}
        self.counts = {}
        for e in ENGS:
            self.counts[e] = 0
            self.counts[e + "d"] = 0
        self.bufs = {}
        self.water = {e: {} for e in ENGS}

    LOOPVARS = frozenset("""par c tc cg tt g4 grp ti k kb half nt s kt ft ft0
        gsz w dt_i bank pair pr hh h sg cc nh2 slot off gpair kg trow gw ntile
        seq_start t0 c0 cw_ r i bk sl""".split())

    def emit(self, eng, fn, reads=(), writes=(), dma=False, inc=True):
        if fn is not None:
            bad = set(fn.__code__.co_freevars) & self.LOOPVARS
            if bad:
                raise RuntimeError(f"lambda captures loop vars {bad} by closure"
                                   f" — bind them as default args")
        sem = eng + "d" if dma else eng
        amt = (16 if dma else 1) if inc else 0
        needs = {}

        def need(psem, pcnt):
            if psem == sem == "tensor":
                return
            if psem.endswith("d"):
                pcnt = self.counts[psem]
            if needs.get(psem, -1) < pcnt:
                needs[psem] = pcnt

        for k in reads:
            b = self.bufs.get(k)
            if b and b["w"] is not None:
                need(*b["w"])
        for k in writes:
            b = self.bufs.get(k)
            if b:
                if b["w"] is not None:
                    need(*b["w"])
                for psem, pcnt in b["r"].items():
                    need(psem, pcnt)
        if dma:
            pre = self.counts[sem]
            if pre > 0 and self.water[eng].get(sem, 0) < pre:
                needs[sem] = max(needs.get(sem, 0), pre)
        waits = []
        for psem in sorted(needs):
            pcnt = needs[psem]
            if self.water[eng].get(psem, 0) < pcnt:
                waits.append((psem, pcnt))
                self.water[eng][psem] = pcnt
        self.counts[sem] += amt
        cnt = self.counts[sem] + (0 if inc else 1)
        self.sched[eng].append((waits, fn, sem, amt))
        for k in writes:
            self.bufs[k] = {"w": (sem, cnt), "r": {}}
        for k in reads:
            b = self.bufs.setdefault(k, {"w": None, "r": {}})
            if b["r"].get(sem, 0) < cnt:
                b["r"][sem] = cnt
        return cnt

    def realize(self, block, sems, tail_waits):
        for e in ENGS:
            items = self.sched[e]
            tw = tail_waits.get(e, [])
            if not items and not tw:
                continue

            def mk(items, tw):
                def body(engobj):
                    for waits, fn, sem, amt in items:
                        for psem, pcnt in waits:
                            if psem.endswith("d"):
                                pcnt = min(pcnt, self.counts[psem])
                            engobj.wait_ge(sems[psem], pcnt)
                        inst = fn(engobj)
                        if amt:
                            inst.then_inc(sems[sem], amt)
                    for psem, pcnt in tw:
                        engobj.wait_ge(sems[psem], pcnt)
                return body
            getattr(block, e)(mk(items, tw))


def ts(i, n=128):
    return slice(i * n, (i + 1) * n)


def build(n_ttiles=NSEQ * NTT_PER_SEQ, ntt_per_seq=NTT_PER_SEQ, dbg=()):
    nc = bass.Bass("TRN2", target_bir_lowering=False)
    NTOK = n_ttiles * TT

    u_d = nc.dram_tensor("u", [NTOK, DM], F32, kind="ExternalInput")
    out_d = nc.dram_tensor("out", [NTOK, DM], F32, kind="ExternalOutput")
    winT_d = nc.dram_tensor("winT", [DM, DIN], F32, kind="ExternalInput")
    w0T_d = nc.dram_tensor("w0T", [DM, 8192], F32, kind="ExternalInput")
    w1T_d = nc.dram_tensor("w1T", [8192, DM], F32, kind="ExternalInput")
    woT_d = nc.dram_tensor("woT", [DI, DM], F32, kind="ExternalInput")
    cwp_d = nc.dram_tensor("cwp", [128, NFT * 4], F32, kind="ExternalInput")
    cbp_d = nc.dram_tensor("cbp", [128, NFT], F32, kind="ExternalInput")
    dtb_d = nc.dram_tensor("dtb", [NH, 1], F32, kind="ExternalInput")
    alog_d = nc.dram_tensor("alog", [NH, 1], F32, kind="ExternalInput")
    dp2_d = nc.dram_tensor("dp2", [128, 32], F32, kind="ExternalInput")
    nw_d = nc.dram_tensor("nw", [128, 32], F32, kind="ExternalInput")
    b0_d = nc.dram_tensor("b0c", [128, 64], F32, kind="ExternalInput")
    b1_d = nc.dram_tensor("b1c", [128, 16], F32, kind="ExternalInput")
    identb_d = nc.dram_tensor("identb", [128, 128], BF16, kind="ExternalInput")
    identf_d = nc.dram_tensor("identf", [128, 128], F32, kind="ExternalInput")
    maskrep_d = nc.dram_tensor("maskrep", [128, 512], F32, kind="ExternalInput")

    wq_0t = nc.dram_tensor("wq_0t", [128, 16 * 2 * 8 * 512], BF16)
    wq_int = nc.dram_tensor("wq_int", [128, 17 * 2 * 8 * 512], BF16)
    wq_1t = nc.dram_tensor("wq_1t", [128, 16 * 2 * 32 * 128], BF16)
    wq_ot = nc.dram_tensor("wq_ot", [128, 32 * 2 * 1024], BF16)
    dab = nc.dram_tensor("dab", [n_ttiles * NCH, NH, Q], F32)
    if "y1" in dbg:
        y1p_d = nc.dram_tensor("dbg_y1", [n_ttiles * NCH, 128, 4096], BF16,
                               kind="ExternalOutput")
    if "EMG" in dbg:
        emp_d = nc.dram_tensor("dbg_EM", [n_ttiles * NCH, 4, 128, 2048], BF16,
                               kind="ExternalOutput")
        gvp_d = nc.dram_tensor("dbg_Gv", [n_ttiles * NCH, 4, 128, 2048], BF16,
                               kind="ExternalOutput")
    gtm_q = nc.dram_tensor("gtm_q", [n_ttiles, TT, DM], BF16)

    p = Prog()
    es = ExitStack()
    sb = lambda name, shape, dt: es.enter_context(
        nc.sbuf_tensor("sb_" + name, shape, dt))

    identb = sb("identb", [128, 128], BF16)
    identf = sb("identf", [128, 128], F32)
    maskrep = sb("maskrep", [128, 512], F32)
    cwp = sb("cwp", [128, NFT * 4], F32)
    cbp = sb("cbp", [128, NFT], F32)
    dp2 = sb("dp2", [128, 32], F32)
    nwc = sb("nwc", [128, 32], F32)
    b0c = sb("b0c", [128, 64], F32)
    b1c = sb("b1c", [128, 16], F32)
    dtb = sb("dtb", [NH, 1], F32)
    alog = sb("alog", [NH, 1], F32)
    acol = sb("acol", [NH, 1], F32)
    ones1f = sb("ones1f", [1, 128], F32)
    onescb = sb("onescb", [128, 1], BF16)
    prevT = sb("prevT", [128, DI], F32)
    wbuf = sb("wbuf", [128, 16384], BF16)
    dtraw = sb("dtraw", [NH, TT], F32)
    dtv4 = sb("dtv4", [NH, TT], F32)
    dacs4 = sb("dacs4", [NH, TT], F32)
    lndt4 = sb("lndt4", [NH, TT], F32)
    wfm4 = sb("wfm4", [NH, TT], F32)
    wfb4 = sb("wfb4", [NH, TT], BF16)
    zq = sb("zq", [NH, Q], F32)
    biasEtm4 = sb("biasEtm4", [128, NCH * NH], F32)
    wtm4 = sb("wtm4", [128, NCH * NH], BF16)
    decN4 = sb("decN4", [128, NCH * NH], F32)
    drow = sb("drow", [1, NH], F32)
    meanc = sb("meanc", [128, 1], F32)
    nmean = sb("nmean", [128, 1], F32)
    sqs = sb("sqs", [128, 1], F32)
    varc = sb("varc", [128, 1], F32)
    rstdc = sb("rstdc", [128, 1], F32)
    rstd_t = sb("rstd_t", [1, 128], F32)
    rstd_c = sb("rstd_c", [128, 2], F32)
    tail = sb("tail", [128, NFT * 3], BF16)

    R1 = sb("R1", [128, 33792], BF16)
    R2 = sb("R2", [128, 32768], BF16)

    def carve(arena, aname, off_b, nelem, dt):
        if dt == BF16:
            ap = arena[:][:, off_b // 2: off_b // 2 + nelem]
        else:
            ap = arena[:].bitcast(F32)[:, off_b // 4: off_b // 4 + nelem]
        nbytes = nelem * (2 if dt == BF16 else 4)
        gran = 4096
        keys = [f"{aname}_{i}" for i in range(off_b // gran, (off_b + nbytes - 1) // gran + 1)]
        return ap, keys

    Dfm, kDfm = carve(R1, "R1", 0, 32768, BF16)
    zsfm, kzs = carve(R1, "R1", 0, 16384, BF16)
    accv, kacc = carve(R1, "R1", 32768, 17408, BF16)
    xnv, kxn = carve(R2, "R2", 0, 8192, BF16)
    ext, kext = carve(R2, "R2", 16384, NFT * (TT + 3), BF16)
    ctmp, kctmp = carve(R2, "R2", 53248, 4608, BF16)
    u_sub_t = sb("u_sub", [128, 2048], F32)
    u_sub, ku_s = u_sub_t[:], ["usub0", "usub1"]
    xn_tm, kxn_tm = carve(R2, "R2", 53248, 2048, BF16)
    gsil, kgsil = carve(R2, "R2", 16384, 8192, BF16)
    x_tm, kx_tm = carve(R2, "R2", 0, 4096, BF16)
    wxv, kwx = carve(R2, "R2", 8192, 4096, BF16)
    ygsq, kygsq = carve(R2, "R2", 8192, 4096, BF16)
    ygv, kyg = [], []
    for i in range(2):
        a, k = carve(R2, "R2", 16384 + i * 8192, 4096, BF16)
        ygv.append(a); kyg.append(k)
    EM, kEM = carve(R2, "R2", 32768, 2048, BF16)
    Gv, kGv = carve(R2, "R2", 36864, 2048, BF16)
    CBb, kCB = carve(R2, "R2", 40960, 128, BF16)
    Btm, kBtm = carve(R2, "R2", 41984, 128, BF16)
    outst = []; kout = []
    for i in range(2):
        a, k = carve(R2, "R2", 43008 + i * 8192, 2048, F32)
        outst.append(a); kout.append(k)
    prevb, kprevb = carve(R2, "R2", 59392, 1024, BF16)
    gtmr, kgtmr = carve(R2, "R2", 61440, 2048, BF16)
    u_rel, ku_rel = carve(R2, "R2", 16384, 2048, F32)

    pb = [es.enter_context(nc.psum_tensor(f"pb{i}", [128, 512], F32)) for i in range(6)]
    pt = [es.enter_context(nc.psum_tensor(f"pt{i}", [128, 1024], BF16)) for i in range(2)]

    def pk(bank, c0, c1):
        return [f"P{bank}_{i}" for i in range(c0 // 128, (c1 - 1) // 128 + 1)] + [
            f"PB{bank}"]

    sems = {}
    for e in ENGS:
        sems[e] = es.enter_context(nc.semaphore("s_" + e))
        sems[e + "d"] = es.enter_context(nc.semaphore("s_" + e + "d"))

    dbg_items = []

    def dump(name, ap, shape, dt, rkeys=()):
        if name in dbg:
            d = nc.dram_tensor("dbg_" + name, shape, dt, kind="ExternalOutput")
            dbg_items.append((name, d, ap, list(rkeys)))

    for key, dst, src in [("identb", identb, identb_d), ("identf", identf, identf_d),
                          ("maskrep", maskrep, maskrep_d),
                          ("cwp", cwp, cwp_d), ("cbp", cbp, cbp_d),
                          ("dp2", dp2, dp2_d), ("nwc", nwc, nw_d),
                          ("b0c", b0c, b0_d), ("b1c", b1c, b1_d),
                          ("dtb", dtb, dtb_d), ("alog", alog, alog_d)]:
        p.emit("gpsimd", lambda g, a=dst, b=src: g.dma_start(a[:], b[:]),
               writes=[key], dma=True)
    p.emit("vector", lambda e: e.memset(ones1f[:], 1.0), writes=["ones1f"])
    p.emit("vector", lambda e: e.memset(onescb[:], 1.0), writes=["onescb"])
    p.emit("vector", lambda e: e.memset(zq[:], 0.0), writes=["zq"])
    p.emit("vector", lambda e: e.memset(prevT[:], 0.0), writes=["prevT"])
    p.emit("vector", lambda e: e.memset(tail[:], 0.0), writes=["tail"])
    p.emit("scalar", lambda e: e.activation(acol[:], alog[:], AF.Exp),
           reads=["alog"], writes=["acol_e"])
    p.emit("vector", lambda e: e.tensor_scalar_mul(acol[:], acol[:], -1.0),
           reads=["acol_e"], writes=["acol"])

    w0v = wq_0t[:].rearrange("p (g h k w) -> p g h k w", g=16, h=2, k=8)
    winv = wq_int[:].rearrange("p (g h k w) -> p g h k w", g=17, h=2, k=8)
    w1v = wq_1t[:].rearrange("p (n h k w) -> p n h k w", n=16, h=2, k=32)
    wov = wq_ot[:].rearrange("p (k n w) -> p k n w", k=32, n=2)

    cast_srcs = [carve(R2, "R2", 32768, 2048, F32),
                 carve(R2, "R2", 40960, 2048, F32)]
    cast_dsts = [carve(R2, "R2", 49152, 2048, BF16),
                 carve(R2, "R2", 53248, 2048, BF16)]
    cast_ctr = [0]

    def cast_piece(qkey, src_ap, dst_ap, dshape=None, fold_col=None):
        i = cast_ctr[0]; cast_ctr[0] += 1
        pr = i % 2
        n = src_ap.shape[1]
        (sap, kcs2), (dap, kcd2) = cast_srcs[pr], cast_dsts[pr]
        s = sap[:, 0:n]
        d = dap[:, 0:n]
        dd = d if dshape is None else d.rearrange(*dshape[0], **dshape[1])
        q = "sync" if pr == 0 else "gpsimd"
        p.emit(q, lambda g, s=s, src_ap=src_ap: g.dma_start(s, src_ap),
               reads=[f"cd{pr}"] + kcd2 + kcs2, writes=[f"cs{pr}"] + kcs2,
               dma=True)
        if fold_col is not None:
            p.emit("vector", lambda e, s=s, d=d, fc=fold_col:
                   e.tensor_scalar_mul(d, s, fc),
                   reads=[f"cs{pr}", "nwc"] + kcs2 + kcd2,
                   writes=[f"cd{pr}"] + kcd2)
        else:
            p.emit("vector", lambda e, s=s, d=d: e.tensor_copy(d, s),
                   reads=[f"cs{pr}"] + kcs2 + kcd2, writes=[f"cd{pr}"] + kcd2)
        p.emit("scalar", lambda g, dst_ap=dst_ap, dd=dd: g.dma_start(
            dst_ap, dd), reads=[f"cd{pr}"] + kcd2, writes=[qkey], dma=True)

    pp_tasks = []

    def pump(n):
        for _ in range(min(n, len(pp_tasks))):
            pp_tasks.pop(0)()

    def _emit_prepass():
        for kg in range(64):
            kh, kk = kg // 32, kg % 32
            pp_tasks.append(lambda kg=kg, kh=kh, kk=kk: cast_piece(
                "wq1_ready", w1T_d[ts(kg), :],
                w1v[:, :, kh, kk, :], (("p (n w) -> p n w",), {"n": 16})))
        for r in range(32):
            pp_tasks.append(lambda r=r: cast_piece(
                "wqo_ready", woT_d[ts(r), :],
                wov[:, r, :, :], (("p (n w) -> p n w",), {"n": 2}),
                fold_col=nwc[:, r:r + 1]))


    def wbk(lo, hi):
        return [f"WB_{i}" for i in range(lo // 1024, (hi - 1) // 1024 + 1)]

    def load_wtile(view_slice, half, rkey, par):
        lo = par * 8192 + half * 4096
        dst = wbuf[:, lo: lo + 4096]
        q = "sync" if half == 0 else "gpsimd"
        p.emit(q, lambda g, dst=dst, s=view_slice: g.dma_start(dst, s),
               reads=[rkey], writes=wbk(lo, lo + 4096), dma=True)

    def wsl(half, kh, blockw, off, length, par):
        base = par * 8192 + half * 4096 + kh * blockw + off
        return wbuf[:, base: base + length], wbk(base, base + length)

    def direct_cast(wT_d, wview, grp, gpar, skey, eng, ncols=512, npad=0):
        col0 = grp * 512 if npad == 0 else 8192
        for r in range(16):
            half, kt = r // 8, r % 8
            pr = r % 2
            s = u_sub[:, pr * 512: pr * 512 + ncols]
            sk = [f"dcS{pr}"]
            base = gpar * 8192 + half * 4096 + kt * 512
            dst = wbuf[:, base: base + ncols]
            dk = wbk(base, base + 512)
            q = "sync" if pr == 0 else "gpsimd"
            p.emit(q, lambda g, s=s, src=wT_d[ts(r), col0: col0 + ncols]:
                   g.dma_start(s, src), reads=sk, writes=sk, dma=True)
            if eng == "vector":
                p.emit(eng, lambda e, s=s, d=dst: e.tensor_copy(d, s),
                       reads=sk + dk, writes=dk + ["dccp"])
            else:
                p.emit(eng, lambda e, s=s, d=dst: e.copy(d, s),
                       reads=sk + dk, writes=dk + ["dccp"])
        for half in range(2):
            lo = gpar * 8192 + half * 4096
            if npad:
                w8 = wbuf[:, lo: lo + 4096].rearrange("p (k w) -> p k w", k=8)
                p.emit("vector", lambda e, w8=w8, npad=npad: e.memset(
                    w8[:, :, 512 - npad: 512], 0.0),
                    reads=wbk(lo, lo + 4096), writes=wbk(lo, lo + 4096))
            p.emit("scalar", lambda g, d=wview[:, grp, half],
                   s=wbuf[:, lo: lo + 4096]: g.dma_start(d, s),
                   reads=wbk(lo, lo + 4096), writes=[skey], dma=True)

    tposed = [0]

    def transpose_b(src_ap, dst_ap, rkeys, wkeys):
        i = tposed[0]; tposed[0] += 1
        bk, sl = (i // 8) % 2, i % 8
        key = f"T{bk}_{sl}"
        outp = pt[bk][0: src_ap.shape[1], sl * 128: sl * 128 + src_ap.shape[0]]
        idn = identb[0:src_ap.shape[0], 0:src_ap.shape[0]]
        p.emit("tensor", lambda e, s=src_ap, d=outp, idn=idn: e.transpose(d, s, idn),
               reads=list(rkeys) + ["identb"], writes=[key, f"TB{bk}"])
        if sl % 2 == 0:
            p.emit("scalar", lambda e, s=outp, d=dst_ap: e.copy(d, s),
                   reads=[key, f"TB{bk}"], writes=list(wkeys))
        else:
            p.emit("vector", lambda e, s=outp, d=dst_ap: e.tensor_copy(d, s),
                   reads=[key, f"TB{bk}"], writes=list(wkeys))

    for tt in range(n_ttiles):
        seq_start = (tt % ntt_per_seq) == 0
        t0 = tt * TT

        for s in range(NSUB):
            p.emit("gpsimd", lambda g, s=s, t0=t0: g.dma_start(
                u_sub, u_d[t0 + s * 128: t0 + (s + 1) * 128, :]),
                reads=ku_s, writes=ku_s + ["u_sub"], dma=True)
            p.emit("vector", lambda e: e.reduce_sum(meanc[:], u_sub, axis=AX.X),
                   reads=ku_s + ["u_sub"], writes=["meanc_s"])
            p.emit("scalar", lambda e: e.activation(
                outst[0], u_sub, AF.Square, accum_out=sqs[:]),
                reads=ku_s + ["u_sub"] + kout[0], writes=kout[0] + ["sqs"])
            p.emit("vector", lambda e: e.tensor_scalar_mul(meanc[:], meanc[:], 1.0 / DM),
                   reads=["meanc_s"], writes=["meanc"])
            p.emit("vector", lambda e: e.tensor_scalar_mul(nmean[:], meanc[:], -1.0),
                   reads=["meanc"], writes=["nmean"])
            p.emit("vector", lambda e: e.tensor_scalar(
                varc[:], sqs[:], 1.0 / DM, EPS, ALU.mult, ALU.add),
                reads=["sqs"], writes=["varc_s"])
            p.emit("vector", lambda e: e.scalar_tensor_tensor(
                varc[:], meanc[:], nmean[:, 0:1], varc[:], ALU.mult, ALU.add),
                reads=["meanc", "nmean", "varc_s"], writes=["varc"])
            p.emit("scalar", lambda e: e.activation(varc[:], varc[:], AF.Sqrt),
                   reads=["varc"], writes=["varc_q"])
            p.emit("vector", lambda e: e.reciprocal(rstdc[:], varc[:]),
                   reads=["varc_q"], writes=["rstdc"])
            p.emit("vector", lambda e: e.tensor_scalar(
                xn_tm, u_sub, meanc[:], rstdc[:], ALU.subtract, ALU.mult),
                reads=ku_s + ["u_sub", "meanc", "rstdc"] + kxn_tm,
                writes=kxn_tm + ["xn_tm"])
            for kt in range(16):
                dst = xnv[:, kt * TT + s * 128: kt * TT + (s + 1) * 128]
                transpose_b(xn_tm[:, ts(kt)], dst, kxn_tm + ["xn_tm"],
                            kxn + ["xnv"])
        dump("xnfm", xnv, [128, 16 * TT], BF16, ["xnv"] + kxn)

        if tt == 0:
            _emit_prepass()

        if tt == 0:
            p.emit("vector", lambda e: e.engine_nop(),
                   writes=["u_sub", "usub0", "usub1", "dcS0", "dcS1"])
        for grp in range(16):
            gpar = grp % 2
            if tt == 0:
                direct_cast(w0T_d, w0v, grp, gpar, "wq0_ready", "vector")
            else:
                for half in range(2):
                    load_wtile(w0v[:, grp, half], half, "wq0_ready", gpar)
            for ti in range(4):
                dt_i = grp * 4 + ti
                bank = dt_i % 3
                bkeys = pk(bank, 0, 512)
                for k in range(16):
                    half, kh = divmod(k, 8)
                    l, lk = wsl(half, kh, 512, ti * 128, 128, gpar)
                    p.emit("tensor", lambda e, l=l, k=k,
                           b=bank: e.matmul(pb[b][:], l, xnv[:, ts(k, TT)],
                                            start=(k == 0), stop=(k == 15)),
                           reads=lk + ["xnv"] + kxn, writes=bkeys,
                           inc=(k == 15))
                ddst = Dfm[:, dt_i * TT: (dt_i + 1) * TT]
                p.emit("scalar", lambda e, d=ddst, b=bank, dt_i=dt_i: e.activation(
                    d, pb[b][:], AF.Gelu, bias=b0c[:, dt_i:dt_i + 1]),
                    reads=bkeys + ["b0c"], writes=kDfm + ["Dfm"])
            if tt == 0:
                pump(4)
        dump("Dfm", Dfm, [128, 64 * TT], BF16, ["Dfm"] + kDfm)

        for nt in range(16):
            bank = nt % 3
            bkeys = pk(bank, 0, 512)
            npar = nt % 2
            for kh in range(2):
                load_wtile(w1v[:, nt, kh], kh, "wq1_ready", npar)
                for k in range(32):
                    kg = kh * 32 + k
                    l, lk = wsl(kh, k, 128, 0, 128, npar)
                    p.emit("tensor", lambda e, l=l, kg=kg,
                           b=bank: e.matmul(pb[b][:], l, Dfm[:, ts(kg, TT)],
                                            start=(kg == 0), stop=(kg == 63)),
                           reads=lk + ["Dfm"] + kDfm, writes=bkeys,
                           inc=(kg == 63))
            gdst = gsil[:, nt * TT: (nt + 1) * TT]
            p.emit("scalar", lambda e, d=gdst, b=bank, nt=nt: e.activation(
                d, pb[b][:], AF.Silu, bias=b1c[:, nt:nt + 1]),
                reads=bkeys + ["b1c"], writes=kgsil + ["gsil"])
            if tt == 0:
                pump(2)
        for s in range(NSUB):
            for nt in range(16):
                transpose_b(gsil[:, nt * TT + s * 128: nt * TT + (s + 1) * 128],
                            gtmr[:, ts(nt)], kgsil + ["gsil"], kgtmr + ["gtmr"])
            p.emit("scalar", lambda g, s=s, tt=tt: g.dma_start(
                gtm_q[tt, s * 128:(s + 1) * 128, :], gtmr),
                reads=kgtmr + ["gtmr"], writes=[f"gq{tt}_{s}"], dma=True)
        dump("gsil", gsil, [128, 16 * TT], BF16, ["gsil"] + kgsil)

        ext3_ = ext.rearrange("p (ft c) -> p ft c", ft=NFT)
        acc3_ = accv.rearrange("p (ft c) -> p ft c", ft=NFT)

        def extk(ft0, gsz):
            lo = 16384 + ft0 * (TT + 3) * 2
            hi = 16384 + (ft0 + gsz) * (TT + 3) * 2
            return [f"R2_{i}" for i in range(lo // 4096, (hi - 1) // 4096 + 1)]

        def acck(ft0, gsz):
            lo = 32768 + ft0 * TT * 2
            hi = 32768 + (ft0 + gsz) * TT * 2
            return [f"R1_{i}" for i in range(lo // 4096, (hi - 1) // 4096 + 1)]

        def conv_group(ft0, gsz, seq_start=seq_start):
            ek = extk(ft0, gsz)
            ak = acck(ft0, gsz)
            if seq_start:
                p.emit("vector", lambda e, ft0=ft0, gsz=gsz: e.memset(
                    ext3_[:, ft0:ft0 + gsz, 0:3], 0.0),
                    reads=ek, writes=ek + ["ext_h"])
            else:
                p.emit("vector", lambda e, ft0=ft0, gsz=gsz: e.tensor_copy(
                    ext3_[:, ft0:ft0 + gsz, 0:3],
                    tail[:].rearrange("p (f c) -> p f c", f=NFT)[:, ft0:ft0 + gsz, :]),
                    reads=["tail"] + ek, writes=ek + ["ext_h"])
            for w in range(4):
                s = ext3_[:, ft0: ft0 + gsz, w: w + TT]
                cw = cwp[:, ft0 * 4 + w: (ft0 + gsz - 1) * 4 + w + 1: 4]
                cwb = cw.unsqueeze(2).broadcast_to([128, gsz, TT])
                ac = acc3_[:, ft0: ft0 + gsz, :]
                if w == 0:
                    p.emit("vector", lambda e, s=s, c=cwb, a=ac: e.tensor_tensor(
                        a, s, c, ALU.mult),
                        reads=ek + ["ext_h", "cwp"] + ak,
                        writes=ak + ["acc_p"])
                else:
                    t3 = ctmp[:, 0: gsz * TT].rearrange("p (f c) -> p f c", f=gsz)
                    p.emit("vector", lambda e, s=s, c=cwb, t=t3: e.tensor_tensor(
                        t, s, c, ALU.mult),
                        reads=ek + ["ext_h", "cwp", "gefence"] + kctmp,
                        writes=kctmp + ["ctmp"])
                    p.emit("vector", lambda e, t=t3, a=ac: e.tensor_tensor(
                        a, a, t, ALU.add),
                        reads=kctmp + ["ctmp", "acc_p"] + ak,
                        writes=ak + ["acc_p"])
            for ft in range(ft0, ft0 + gsz):
                a = accv[:, ft * TT: (ft + 1) * TT]
                p.emit("scalar", lambda e, a=a, ft=ft: e.activation(
                    a, a, AF.Silu, bias=cbp[:, ft:ft + 1]),
                    reads=["acc_p", "cbp"] + ak, writes=ak + ["accv"])

        def _emit_dt_hoist():
            p.emit("vector", lambda e: e.tensor_scalar_add(
                dtraw[:], dtraw[:], dtb[:, 0:1]),
                reads=["dtraw", "dtb"], writes=["dtraw"])
            p.emit("scalar", lambda e: e.activation(dtv4[:], dtraw[:], AF.Abs),
                   reads=["dtraw"], writes=["dt_abs"])
            p.emit("scalar", lambda e: e.activation(dtv4[:], dtv4[:], AF.Exp, scale=-1.0),
                   reads=["dt_abs"], writes=["dt_exp"])
            p.emit("vector", lambda e: e.tensor_scalar_add(dtv4[:], dtv4[:], 1.0),
                   reads=["dt_exp"], writes=["dt_e1"])
            p.emit("scalar", lambda e: e.activation(dtv4[:], dtv4[:], AF.Ln),
                   reads=["dt_e1"], writes=["dt_l1p"])
            p.emit("vector", lambda e: e.tensor_scalar_max(dacs4[:], dtraw[:], 0.0),
                   reads=["dtraw"], writes=["dt_mx"])
            p.emit("vector", lambda e: e.tensor_add(dtv4[:], dtv4[:], dacs4[:]),
                   reads=["dt_mx", "dt_l1p"], writes=["dtv4"])
            p.emit("vector", lambda e: e.tensor_scalar_mul(
                dtraw[:], dtv4[:], acol[:, 0:1]),
                reads=["dtv4", "acol", "dtraw"], writes=["dtA4"])
            for c in range(NCH):
                cg = tt * NCH + c
                tc = c * Q
                p.emit("vector", lambda e, tc=tc: e.tensor_tensor_scan(
                    dacs4[:, tc:tc + Q], dtraw[:, tc:tc + Q], zq[:], 0.0,
                    ALU.add, ALU.add),
                    reads=["dtA4", "zq", "dt_mx"], writes=["dacs4"])
                p.emit("gpsimd", lambda g, cg=cg, tc=tc: g.dma_start(
                    dab[cg], dacs4[:, tc:tc + Q]),
                    reads=["dacs4"], writes=[f"dab{cg}"], dma=True)
            p.emit("scalar", lambda e: e.activation(lndt4[:], dtv4[:], AF.Ln),
                   reads=["dtv4"], writes=["lndt4"])
            p.emit("vector", lambda e: e.tensor_sub(lndt4[:], lndt4[:], dacs4[:]),
                   reads=["lndt4", "dacs4"], writes=["biasE4"])
            for c in range(NCH):
                tc = c * Q
                p.emit("scalar", lambda e, tc=tc: e.activation(
                    wfm4[:, tc:tc + Q], dacs4[:, tc:tc + Q], AF.Exp,
                    bias=dacs4[:, tc + Q - 1:tc + Q], scale=-1.0),
                    reads=["dacs4"], writes=["wfm4e"])
            p.emit("vector", lambda e: e.tensor_mul(wfm4[:], wfm4[:], dtv4[:]),
                   reads=["wfm4e", "dtv4"], writes=["wfm4"])
            p.emit("vector", lambda e: e.tensor_copy(wfb4[:], wfm4[:]),
                   reads=["wfm4"], writes=["wfb4"])
            for c in range(NCH):
                tc = c * Q
                p.emit("tensor", lambda e, tc=tc: e.transpose(
                    pb[5][:, 0:64], lndt4[:, tc:tc + Q], identf[0:64, 0:64]),
                    reads=["biasE4", "identf"], writes=pk(5, 0, 128))
                p.emit("vector", lambda e, c=c: e.tensor_copy(
                    biasEtm4[:, c * NH:(c + 1) * NH], pb[5][:, 0:64]),
                    reads=pk(5, 0, 128), writes=["biasEtm4"])
                transpose_b(wfb4[:, tc:tc + Q], wtm4[:, c * NH:(c + 1) * NH],
                            ["wfb4"], ["wtm4"])
                p.emit("tensor", lambda e, tc=tc: e.matmul(
                    pb[5][0:1, 128:192], dacs4[:, tc + Q - 1:tc + Q],
                    identf[0:64, 0:64], start=True, stop=True),
                    reads=["dacs4", "identf"], writes=pk(5, 128, 192))
                p.emit("vector", lambda e: e.tensor_copy(drow[:], pb[5][0:1, 128:192]),
                       reads=pk(5, 128, 192), writes=["drow"])
                p.emit("tensor", lambda e: e.matmul(
                    pb[5][:, 192:256], ones1f[:], drow[:], start=True, stop=True),
                    reads=["drow", "ones1f"], writes=pk(5, 192, 256))
                p.emit("scalar", lambda e, c=c: e.activation(
                    decN4[:, c * NH:(c + 1) * NH], pb[5][:, 192:256], AF.Exp),
                    reads=pk(5, 192, 256), writes=["decN4"])


        conv_sched = {10: (0, 9), 12: (9, 9), 14: (18, 8), 15: (26, 8)}
        p.emit("vector", lambda e: e.engine_nop(),
               reads=["gv0", "gv1", "em0", "em1", "pvb0", "pvb1",
                      "bcast0", "bcast1", "EM", "Gv"],
               writes=["gefence"])
        if tt == 0:
            pump(len(pp_tasks))
        for grp in [16] + list(range(16)):
            ntile = 4 if grp < 16 else 3
            gpar = grp % 2
            if tt == 0:
                direct_cast(winT_d, winv, grp, gpar, "wqi_ready", "scalar",
                            ncols=512 if grp < 16 else 320,
                            npad=0 if grp < 16 else 192)
            else:
                for half in range(2):
                    load_wtile(winv[:, grp, half], half, "wqi_ready", gpar)
            for ti in range(ntile):
                dt_i = grp * 4 + ti
                if dt_i == 66:
                    bkeys = pk(4, 0, 512)
                    for k in range(16):
                        half, kh = divmod(k, 8)
                        l, lk = wsl(half, kh, 512, ti * 128, 64, gpar)
                        p.emit("tensor", lambda e, l=l, k=k: e.matmul(
                            pb[4][0:64, :], l, xnv[:, ts(k, TT)],
                            start=(k == 0), stop=(k == 15)),
                            reads=lk + ["xnv"] + kxn, writes=bkeys,
                            inc=(k == 15))
                    p.emit("vector", lambda e: e.tensor_copy(
                        dtraw[:], pb[4][0:64, :]),
                        reads=bkeys, writes=["dtraw"])
                    _emit_dt_hoist()
                    continue
                bank = dt_i % 3
                bkeys = pk(bank, 0, 512)
                for k in range(16):
                    half, kh = divmod(k, 8)
                    l, lk = wsl(half, kh, 512, ti * 128, 128, gpar)
                    p.emit("tensor", lambda e, l=l, k=k,
                           b=bank: e.matmul(pb[b][:], l, xnv[:, ts(k, TT)],
                                            start=(k == 0), stop=(k == 15)),
                           reads=lk + ["xnv"] + kxn, writes=bkeys,
                           inc=(k == 15))
                if dt_i < 32:
                    zdst = zsfm[:, dt_i * TT: (dt_i + 1) * TT]
                    p.emit("scalar", lambda e, d=zdst, b=bank: e.activation(
                        d, pb[b][:], AF.Silu), reads=bkeys,
                        writes=kzs + ["zsfm"])
                else:
                    ft = dt_i - 32
                    edst = ext[:, ft * (TT + 3) + 3: ft * (TT + 3) + 3 + TT]
                    p.emit("vector", lambda e, d=edst, b=bank: e.tensor_copy(
                        d, pb[b][:]), reads=bkeys + ["gefence"],
                        writes=extk(ft, 1) + ["ext"])
            if grp in conv_sched:
                conv_group(*conv_sched[grp])
        if tt == 0:
            p.emit("vector", lambda e: e.engine_nop(),
                   reads=["dccp"],
                   writes=["u_sub", "usub0", "usub1"])
        if "xne" in dbg and tt == 0:
            xned = nc.dram_tensor("dbg_xne", [128, 16 * TT], BF16,
                                  kind="ExternalOutput")
            p.emit("gpsimd", lambda g: g.dma_start(xned[:], xnv),
                   reads=["xnv"] + kxn, writes=["xnedump"], dma=True)
        dump("zsfm", zsfm, [128, 32 * TT], BF16, ["zsfm"] + kzs)
        dump("dtraw", dtraw[:], [NH, TT], F32, ["dtraw"])

        ext3 = ext.rearrange("p (ft c) -> p ft c", ft=NFT)
        p.emit("vector", lambda e, ext3=ext3: e.tensor_copy(
            tail[:].rearrange("p (ft c) -> p ft c", ft=NFT), ext3[:, :, TT: TT + 3]),
            reads=kext + ["ext", "ext_h"], writes=["tail"])
        dump("accv", accv, [128, NFT * TT], BF16, ["accv"] + kacc)

        p.emit("vector", lambda e: e.engine_nop(),
               reads=kGv + kEM + kprevb + ["ext", "accv"],
               writes=["ssdfence"])
        for c in range(NCH):
            cg = tt * NCH + c
            tc = c * Q
            par = c % 2
            Bfm = accv[:, 32 * TT + tc: 32 * TT + tc + Q]
            Cfm = accv[:, 33 * TT + tc: 33 * TT + tc + Q]
            transpose_b(Bfm, Btm, kacc + ["accv"], kBtm + ["Btm"])
            for ftx in range(32):
                transpose_b(accv[:, ftx * TT + tc: ftx * TT + tc + Q],
                            x_tm[:, ts(ftx)], kacc + ["accv"], kx_tm + ["x_tm"])
            wtm_b = wtm4[:, c * NH:(c + 1) * NH].unsqueeze(2).broadcast_to(
                [128, NH, DH])
            p.emit("vector", lambda e, wb=wtm_b: e.tensor_tensor(
                wxv.rearrange("p (h d) -> p h d", h=NH),
                x_tm.rearrange("p (h d) -> p h d", h=NH), wb, ALU.mult),
                reads=kx_tm + ["x_tm", "wtm4"] + kwx, writes=kwx + ["wxv"])
            p.emit("tensor", lambda e, B=Bfm, C=Cfm: e.matmul(
                pb[4][:, 0:128], B, C, start=True, stop=True),
                reads=kacc + ["accv"], writes=pk(4, 0, 128))
            p.emit("vector", lambda e: e.tensor_copy(CBb, pb[4][:, 0:128]),
                   reads=pk(4, 0, 128), writes=kCB + ["CBb"])

            def pre_dma(s8):
                sp = s8 % 2
                uhalf = u_sub[:, sp * 1024:(sp + 1) * 1024]
                kb = f"usub{sp}"
                bsrc = dab[cg].rearrange("h q -> (h q)").unsqueeze(0)[
                    0:1, s8 * 1024: (s8 + 1) * 1024].partition_broadcast(128)
                p.emit("gpsimd", lambda g, bsrc=bsrc, uhalf=uhalf: g.dma_start(
                    uhalf, bsrc),
                    reads=[f"dab{cg}", kb], writes=[kb, f"bcast{sp}"], dma=True)

            def pre_exp(s8):
                sp = s8 % 2
                uhalf = u_sub[:, sp * 1024:(sp + 1) * 1024]
                gvh = Gv[:, sp * 1024:(sp + 1) * 1024]
                kb = f"usub{sp}"; kg_ = f"gv{sp}"
                p.emit("scalar", lambda e, gvh=gvh, uhalf=uhalf: e.activation(
                    gvh, uhalf, AF.Exp),
                    reads=[kb, f"bcast{sp}", "ssdfence"],
                    writes=[kg_, kg_ + "g"])

            def ssd_front(s8):
                sp = s8 % 2
                h0 = s8 * 8
                uhalf = u_sub[:, sp * 1024:(sp + 1) * 1024]
                gvh = Gv[:, sp * 1024:(sp + 1) * 1024]
                emh = EM[:, sp * 1024:(sp + 1) * 1024]
                pvh = prevb[:, sp * 512:(sp + 1) * 512]
                kb = f"usub{sp}"; kg_ = f"gv{sp}"; ke_ = f"em{sp}"
                kp_ = f"pvb{sp}"
                u2v = uhalf.rearrange("p (o w) -> p o w", o=2)
                mrep = maskrep[:].unsqueeze(1).broadcast_to([128, 2, 512])
                p.emit("vector", lambda e, u2v=u2v, mrep=mrep: e.tensor_tensor(
                    u2v, u2v, mrep, ALU.add),
                    reads=["maskrep", kg_, f"bcast{sp}", kb],
                    writes=[kb, f"bcastm{sp}"])
                u8v = uhalf.rearrange("p (h w) -> p h w", h=8)
                bsl = biasEtm4[:, c * NH + h0: c * NH + h0 + 8]
                bslb = bsl.unsqueeze(2).broadcast_to([128, 8, Q])
                p.emit("vector", lambda e, u8v=u8v, bslb=bslb: e.tensor_tensor(
                    u8v, u8v, bslb, ALU.add),
                    reads=["biasEtm4", f"bcastm{sp}", kb],
                    writes=[kb, f"bcastb{sp}"])
                p.emit("scalar", lambda e, emh=emh, uhalf=uhalf: e.activation(
                    emh, uhalf, AF.Exp),
                    reads=[kb, f"bcastb{sp}", "ssdfence"],
                    writes=[ke_ + "e", ke_])
                cb = Cfm.unsqueeze(1).broadcast_to([128, 8, Q])
                gv3 = gvh.rearrange("p (h q) -> p h q", h=8)
                p.emit("vector", lambda e, gv3=gv3, cb=cb: e.tensor_tensor(
                    gv3, gv3, cb, ALU.mult),
                    reads=[kg_, "accv"] + kacc, writes=[kg_ + "g"])
                cbb = CBb.unsqueeze(1).broadcast_to([128, 8, Q])
                em3 = emh.rearrange("p (h q) -> p h q", h=8)
                p.emit("vector", lambda e, em3=em3, cbb=cbb: e.tensor_tensor(
                    em3, em3, cbb, ALU.mult),
                    reads=[ke_ + "e", "CBb"] + kCB, writes=[ke_])
                p.emit("scalar", lambda e, pvh=pvh, h0=h0: e.copy(
                    pvh, prevT[:, h0 * DH: (h0 + 8) * DH]),
                    reads=["prevT", kp_], writes=[kp_])

            def ssd_back(s8):
                sp = s8 % 2
                h0 = s8 * 8
                gvh = Gv[:, sp * 1024:(sp + 1) * 1024]
                emh = EM[:, sp * 1024:(sp + 1) * 1024]
                pvh = prevb[:, sp * 512:(sp + 1) * 512]
                kg_ = f"gv{sp}"; ke_ = f"em{sp}"; kp_ = f"pvb{sp}"
                ybank = 4 + sp
                for hh in range(8):
                    h = h0 + hh
                    pair, pr = divmod(hh, 2)
                    off = pair * 128
                    yp = pb[ybank][pr * 64:(pr + 1) * 64, off:off + 128]
                    ykeys = pk(ybank, off, off + 128)
                    p.emit("tensor", lambda e, yp=yp, hh=hh, pvh=pvh, gvh=gvh:
                        e.matmul(yp, pvh[:, ts(hh, DH)], gvh[:, ts(hh)],
                                 start=True, stop=False),
                        reads=[kp_, kg_ + "g"], writes=ykeys, inc=False)
                    p.emit("tensor", lambda e, yp=yp, h=h, hh=hh, emh=emh:
                        e.matmul(yp, x_tm[:, ts(h, DH)], emh[:, ts(hh)],
                                 start=False, stop=True),
                        reads=kx_tm + ["x_tm", ke_], writes=ykeys)
                for pair in range(4):
                    gpair = s8 * 4 + pair
                    off = pair * 128
                    yp = pb[ybank][:, off:off + 128]
                    xsl = accv[:, gpair * TT + tc: gpair * TT + tc + Q]
                    p.emit("vector", lambda e, yp=yp, xsl=xsl, gpair=gpair,
                           par=par:
                        e.scalar_tensor_tensor(
                            ygv[par][:, ts(gpair)], xsl, dp2[:, gpair:gpair + 1],
                            yp, ALU.mult, ALU.add),
                        reads=pk(ybank, off, off + 128) + kacc +
                        ["accv", "dp2"] + kyg[par],
                        writes=kyg[par] + [f"yg{par}"])
                p.emit("tensor", lambda e, sp=sp, s8=s8: e.matmul(
                    pb[sp][:], Btm, wxv[:, s8 * 512: (s8 + 1) * 512],
                    start=True, stop=True),
                    reads=kBtm + ["Btm", "wxv"] + kwx, writes=pk(sp, 0, 512))
                dsl = decN4[:, c * NH + h0: c * NH + h0 + 8]
                db = dsl.unsqueeze(2).broadcast_to([128, 8, DH])
                pv = prevT[:, h0 * DH: (h0 + 8) * DH]
                pv3 = pv.rearrange("p (h d) -> p h d", h=8)
                p.emit("gpsimd", lambda e, pv3=pv3, db=db: e.tensor_tensor(
                    pv3, pv3, db, ALU.mult),
                    reads=["prevT", "decN4", kp_], writes=["prevT_s"])
                p.emit("vector", lambda e, pv=pv, sp=sp: e.tensor_tensor(
                    pv, pv, pb[sp][:, 0:512], ALU.add),
                    reads=["prevT_s"] + pk(sp, 0, 512), writes=["prevT"])


            pre_dma(0)
            pre_exp(0)
            for s8 in range(9):
                if s8 + 1 < 8:
                    pre_dma(s8 + 1)
                if s8 < 8:
                    ssd_front(s8)
                if s8 >= 1:
                    ssd_back(s8 - 1)
                if s8 + 1 < 8:
                    pre_exp(s8 + 1)

            if "y1" in dbg:
                p.emit("gpsimd", lambda g, cg=cg, par=par: g.dma_start(
                    y1p_d[cg], ygv[par]), reads=kyg[par] + [f"yg{par}"],
                    writes=["y1pdbg"], dma=True)
            zsl = zsfm.rearrange("p (f c) -> p f c", f=32)[:, :, tc:tc + Q]
            y3 = ygv[par].rearrange("p (f c) -> p f c", f=32)
            p.emit("vector", lambda e, zsl=zsl, y3=y3: e.tensor_tensor(
                y3, y3, zsl, ALU.mult),
                reads=[f"yg{par}", "zsfm"] + kzs + kyg[par],
                writes=kyg[par] + [f"yg{par}"])
            p.emit("scalar", lambda e, par=par: e.activation(
                ygsq, ygv[par], AF.Square),
                reads=[f"yg{par}"] + kyg[par] + kygsq, writes=kygsq + ["ygsq"])
            for ftx in range(32):
                p.emit("tensor", lambda e, ftx=ftx: e.matmul(
                    pb[4][0:1, 256:384], onescb[:], ygsq[:, ts(ftx)],
                    start=(ftx == 0), stop=(ftx == 31)),
                    reads=kygsq + ["ygsq", "onescb"], writes=pk(4, 256, 384),
                    inc=(ftx == 31))
            p.emit("vector", lambda e: e.tensor_scalar(
                rstd_t[:], pb[4][0:1, 256:384], 1.0 / DI, EPS, ALU.mult, ALU.add),
                reads=pk(4, 256, 384), writes=["rstd_ts"])
            p.emit("scalar", lambda e: e.activation(
                rstd_t[:], rstd_t[:], AF.Ln),
                reads=["rstd_ts"], writes=["rstd_ln"])
            p.emit("scalar", lambda e: e.activation(
                rstd_t[:], rstd_t[:], AF.Exp, scale=-0.5),
                reads=["rstd_ln"], writes=["rstd_row"])
            p.emit("tensor", lambda e: e.matmul(
                pb[4][:, 384:385], rstd_t[:], ones1f[0:1, 0:1],
                start=True, stop=True),
                reads=["rstd_row", "ones1f"], writes=pk(4, 384, 512))
            p.emit("vector", lambda e, par=par: e.tensor_copy(
                rstd_c[:, par:par + 1], pb[4][:, 384:385]),
                reads=pk(4, 384, 512), writes=[f"rstd{par}"])

            if par == 1:
                for nh2 in range(2):
                    for k in range(32):
                        slot = k % 8
                        wosl = wbuf[:, slot * 2048: slot * 2048 + 1024]
                        wok = wbk(slot * 2048, slot * 2048 + 1024)
                        p.emit("sync",
                               lambda g, k=k, w=wosl, nh2=nh2: g.dma_start(
                                   w, wov[:, k, nh2]),
                               reads=["wqo_ready"], writes=wok, dma=True)
                        for cc in range(2):
                            for sg in range(2):
                                bank = cc * 2 + sg
                                p.emit("tensor", lambda e, k=k, w=wosl, cc=cc,
                                       sg=sg, bank=bank: e.matmul(
                                    pb[bank][:], ygv[cc][:, ts(k)],
                                    w[:, ts(sg, 512)],
                                    start=(k == 0), stop=(k == 31)),
                                    reads=wok + [f"yg{cc}"] + kyg[cc],
                                    writes=pk(bank, 0, 512),
                                    inc=(k % 4 == 3))
                    for cc in range(2):
                        for sg in range(2):
                            bank = cc * 2 + sg
                            dsl = outst[cc][:, nh2 * 1024 + sg * 512:
                                            nh2 * 1024 + (sg + 1) * 512]
                            p.emit("vector", lambda e, d=dsl, bank=bank, cc=cc:
                                e.tensor_scalar_mul(d, pb[bank][:],
                                                    rstd_c[:, cc:cc + 1]),
                                reads=pk(bank, 0, 512) + [f"rstd{cc}"] + kout[cc],
                                writes=kout[cc] + [f"outst{cc}"])
                for cc in range(2):
                    trow = t0 + (c - 1 + cc) * Q
                    p.emit("scalar", lambda g, tt=tt, cc=cc, c=c: g.dma_start(
                        gtmr, gtm_q[tt, (c - 1 + cc) * Q: (c + cc) * Q, :]),
                        reads=[f"gq{tt}_{c - 1 + cc}"] + kgtmr,
                        writes=kgtmr + ["gtmr"], dma=True)
                    p.emit("scalar", lambda g, trow=trow: g.dma_start(
                        u_rel, u_d[trow: trow + Q, :]),
                        reads=ku_rel, writes=ku_rel + ["u_rel"], dma=True)
                    p.emit("vector", lambda e, cc=cc: e.tensor_tensor(
                        outst[cc], outst[cc], gtmr, ALU.mult),
                        reads=[f"outst{cc}", "gtmr"] + kgtmr + kout[cc],
                        writes=kout[cc] + [f"outst{cc}"])
                    p.emit("vector", lambda e, cc=cc: e.tensor_tensor(
                        outst[cc], outst[cc], u_rel, ALU.add),
                        reads=[f"outst{cc}", "u_rel"] + ku_rel + kout[cc],
                        writes=kout[cc] + [f"outst{cc}"])
                    p.emit("gpsimd", lambda g, trow=trow, cc=cc: g.dma_start(
                        out_d[trow: trow + Q, :], outst[cc]),
                        reads=[f"outst{cc}"] + kout[cc], writes=[f"od{trow}"],
                        dma=True)

    for name, d, ap, rkeys in dbg_items:
        p.emit("gpsimd", lambda g, d=d, ap=ap: g.dma_start(d[:], ap),
               reads=rkeys, writes=[f"dbg{name}"], dma=True)

    tail_waits = {"gpsimd": [("gpsimdd", p.counts["gpsimdd"]),
                             ("syncd", p.counts["syncd"]),
                             ("scalard", p.counts["scalard"])]}
    with nc.Block() as block:
        p.realize(block, sems, tail_waits)
    es.close()
    return nc


def _host_inputs(inputs, core, n_ttiles=NSEQ * NTT_PER_SEQ):
    u = inputs["L0_attn_u"]
    W_in = inputs["L0_attn_W_in"]
    conv_w = inputs["L0_attn_conv_w"]
    conv_b = inputs["L0_attn_conv_b"]
    dt_bias = inputs["L0_attn_dt_bias"]
    A_log = inputs["L0_attn_A_log"]
    Dp = inputs["L0_attn_Dp"]
    norm_w = inputs["L0_attn_norm_w"]
    W_out = inputs["L0_attn_W_out"]
    W0 = inputs["L0_ffn_W0"]
    b0 = inputs["L0_ffn_bias0"]
    W1 = inputs["L0_ffn_W1"]
    b1 = inputs["L0_ffn_bias1"]

    dp2 = np.empty((128, 32), np.float32)
    for g in range(32):
        dp2[:64, g] = Dp[2 * g]
        dp2[64:, g] = Dp[2 * g + 1]
    d = {
        "u": np.ascontiguousarray(u[2 * core: 2 * core + 2].reshape(NSEQ * SEQ, DM)[
            : n_ttiles * TT]),
        "winT": np.ascontiguousarray(W_in.T),
        "w0T": np.ascontiguousarray(W0.T),
        "w1T": np.ascontiguousarray(W1.T),
        "woT": np.ascontiguousarray(W_out.T),
        "cwp": np.ascontiguousarray(
            conv_w.reshape(NFT, 128, 4).transpose(1, 0, 2).reshape(128, NFT * 4)),
        "cbp": np.ascontiguousarray(conv_b.reshape(NFT, 128).T),
        "dtb": np.ascontiguousarray(dt_bias.reshape(NH, 1)),
        "alog": np.ascontiguousarray(A_log.reshape(NH, 1)),
        "dp2": dp2,
        "nw": np.ascontiguousarray(norm_w.reshape(32, 128).T),
        "b0c": np.ascontiguousarray(b0.reshape(64, 128).T),
        "b1c": np.ascontiguousarray(b1.reshape(16, 128).T),
        "identb": np.eye(128, dtype=BFD),
        "identf": np.eye(128, dtype=np.float32),
        "maskrep": np.tile(np.tril(np.full((128, 128), -1e38, np.float32), -1),
                           (1, 4)),
    }
    return d


def kernel(**inputs):
    nc = build()
    in_maps = [_host_inputs(inputs, c) for c in range(8)]
    res = run_bass_kernel_spmd(nc, in_maps, core_ids=list(range(8)))
    out = np.stack([r["out"].reshape(NSEQ, SEQ, DM) for r in res.results])
    return np.ascontiguousarray(out.reshape(16, SEQ, DM)).astype(np.float32)


# version 16
# speedup vs baseline: 1.0062x; 1.0062x over previous
import numpy as np
import ml_dtypes
from contextlib import ExitStack
import concourse.bass as bass
import concourse.mybir as mybir
from concourse.bass_utils import run_bass_kernel_spmd

F32 = mybir.dt.float32
BF16 = mybir.dt.bfloat16
AF = mybir.ActivationFunctionType
ALU = mybir.AluOpType
AX = mybir.AxisListType
BFD = ml_dtypes.bfloat16

DM = 2048
DIN = 8512
DI = 4096
DS = 128
NH = 64
DH = 64
CD = 4352
NFT = 34
Q = 128
TT = 512
NSUB = TT // 128
NCH = TT // Q
SEQ = 2048
NSEQ = 2
NTT_PER_SEQ = SEQ // TT
EPS = 1e-5

ENGS = ("tensor", "vector", "scalar", "gpsimd", "sync")


class Prog:

    DMA_WINDOW = 8

    def __init__(self):
        self.sched = {e: [] for e in ENGS}
        self.counts = {}
        for e in ENGS:
            self.counts[e] = 0
            self.counts[e + "d"] = 0
        self.bufs = {}
        self.water = {e: {} for e in ENGS}

    LOOPVARS = frozenset("""par c tc cg tt g4 grp ti k kb half nt s kt ft ft0
        gsz w dt_i bank pair pr hh h sg cc nh2 slot off gpair kg trow gw ntile
        seq_start t0 c0 cw_ r i bk sl""".split())

    def emit(self, eng, fn, reads=(), writes=(), dma=False, inc=True):
        if fn is not None:
            bad = set(fn.__code__.co_freevars) & self.LOOPVARS
            if bad:
                raise RuntimeError(f"lambda captures loop vars {bad} by closure"
                                   f" — bind them as default args")
        sem = eng + "d" if dma else eng
        amt = (16 if dma else 1) if inc else 0
        needs = {}

        def need(psem, pcnt):
            if psem == sem == "tensor":
                return
            if psem.endswith("d"):
                pcnt = self.counts[psem]
            if needs.get(psem, -1) < pcnt:
                needs[psem] = pcnt

        for k in reads:
            b = self.bufs.get(k)
            if b and b["w"] is not None:
                need(*b["w"])
        for k in writes:
            b = self.bufs.get(k)
            if b:
                if b["w"] is not None:
                    need(*b["w"])
                for psem, pcnt in b["r"].items():
                    need(psem, pcnt)
        if dma:
            pre = self.counts[sem]
            if pre > 0 and self.water[eng].get(sem, 0) < pre:
                needs[sem] = max(needs.get(sem, 0), pre)
        waits = []
        for psem in sorted(needs):
            pcnt = needs[psem]
            if self.water[eng].get(psem, 0) < pcnt:
                waits.append((psem, pcnt))
                self.water[eng][psem] = pcnt
        self.counts[sem] += amt
        cnt = self.counts[sem] + (0 if inc else 1)
        self.sched[eng].append((waits, fn, sem, amt))
        for k in writes:
            self.bufs[k] = {"w": (sem, cnt), "r": {}}
        for k in reads:
            b = self.bufs.setdefault(k, {"w": None, "r": {}})
            if b["r"].get(sem, 0) < cnt:
                b["r"][sem] = cnt
        return cnt

    def realize(self, block, sems, tail_waits):
        for e in ENGS:
            items = self.sched[e]
            tw = tail_waits.get(e, [])
            if not items and not tw:
                continue

            def mk(items, tw):
                def body(engobj):
                    for waits, fn, sem, amt in items:
                        for psem, pcnt in waits:
                            if psem.endswith("d"):
                                pcnt = min(pcnt, self.counts[psem])
                            engobj.wait_ge(sems[psem], pcnt)
                        inst = fn(engobj)
                        if amt:
                            inst.then_inc(sems[sem], amt)
                    for psem, pcnt in tw:
                        engobj.wait_ge(sems[psem], pcnt)
                return body
            getattr(block, e)(mk(items, tw))


def ts(i, n=128):
    return slice(i * n, (i + 1) * n)


def build(n_ttiles=NSEQ * NTT_PER_SEQ, ntt_per_seq=NTT_PER_SEQ, dbg=()):
    nc = bass.Bass("TRN2", target_bir_lowering=False)
    NTOK = n_ttiles * TT

    u_d = nc.dram_tensor("u", [NTOK, DM], F32, kind="ExternalInput")
    out_d = nc.dram_tensor("out", [NTOK, DM], F32, kind="ExternalOutput")
    winT_d = nc.dram_tensor("winT", [DM, DIN], F32, kind="ExternalInput")
    w0T_d = nc.dram_tensor("w0T", [DM, 8192], F32, kind="ExternalInput")
    w1T_d = nc.dram_tensor("w1T", [8192, DM], F32, kind="ExternalInput")
    woT_d = nc.dram_tensor("woT", [DI, DM], F32, kind="ExternalInput")
    cwp_d = nc.dram_tensor("cwp", [128, NFT * 4], F32, kind="ExternalInput")
    cbp_d = nc.dram_tensor("cbp", [128, NFT], F32, kind="ExternalInput")
    dtb_d = nc.dram_tensor("dtb", [NH, 1], F32, kind="ExternalInput")
    alog_d = nc.dram_tensor("alog", [NH, 1], F32, kind="ExternalInput")
    dp2_d = nc.dram_tensor("dp2", [128, 32], F32, kind="ExternalInput")
    nw_d = nc.dram_tensor("nw", [128, 32], F32, kind="ExternalInput")
    b0_d = nc.dram_tensor("b0c", [128, 64], F32, kind="ExternalInput")
    b1_d = nc.dram_tensor("b1c", [128, 16], F32, kind="ExternalInput")
    identb_d = nc.dram_tensor("identb", [128, 128], BF16, kind="ExternalInput")
    identf_d = nc.dram_tensor("identf", [128, 128], F32, kind="ExternalInput")
    maskrep_d = nc.dram_tensor("maskrep", [128, 512], F32, kind="ExternalInput")

    wq_0t = nc.dram_tensor("wq_0t", [128, 16 * 2 * 8 * 512], BF16)
    wq_int = nc.dram_tensor("wq_int", [128, 17 * 2 * 8 * 512], BF16)
    wq_1t = nc.dram_tensor("wq_1t", [128, 16 * 2 * 32 * 128], BF16)
    wq_ot = nc.dram_tensor("wq_ot", [128, 32 * 2 * 1024], BF16)
    dab = nc.dram_tensor("dab", [n_ttiles * NCH, NH, Q], F32)
    if "y1" in dbg:
        y1p_d = nc.dram_tensor("dbg_y1", [n_ttiles * NCH, 128, 4096], BF16,
                               kind="ExternalOutput")
    if "EMG" in dbg:
        emp_d = nc.dram_tensor("dbg_EM", [n_ttiles * NCH, 4, 128, 2048], BF16,
                               kind="ExternalOutput")
        gvp_d = nc.dram_tensor("dbg_Gv", [n_ttiles * NCH, 4, 128, 2048], BF16,
                               kind="ExternalOutput")
    gtm_q = nc.dram_tensor("gtm_q", [n_ttiles, TT, DM], BF16)

    p = Prog()
    es = ExitStack()
    sb = lambda name, shape, dt: es.enter_context(
        nc.sbuf_tensor("sb_" + name, shape, dt))

    identb = sb("identb", [128, 128], BF16)
    identf = sb("identf", [128, 128], F32)
    maskrep = sb("maskrep", [128, 512], F32)
    cwp = sb("cwp", [128, NFT * 4], F32)
    cbp = sb("cbp", [128, NFT], F32)
    dp2 = sb("dp2", [128, 32], F32)
    nwc = sb("nwc", [128, 32], F32)
    b0c = sb("b0c", [128, 64], F32)
    b1c = sb("b1c", [128, 16], F32)
    dtb = sb("dtb", [NH, 1], F32)
    alog = sb("alog", [NH, 1], F32)
    acol = sb("acol", [NH, 1], F32)
    ones1f = sb("ones1f", [1, 128], F32)
    onescb = sb("onescb", [128, 1], BF16)
    prevT = sb("prevT", [128, DI], F32)
    wbuf = sb("wbuf", [128, 16384], BF16)
    dtraw = sb("dtraw", [NH, TT], F32)
    dtv4 = sb("dtv4", [NH, TT], F32)
    dacs4 = sb("dacs4", [NH, TT], F32)
    lndt4 = sb("lndt4", [NH, TT], F32)
    wfm4 = sb("wfm4", [NH, TT], F32)
    wfb4 = sb("wfb4", [NH, TT], BF16)
    zq = sb("zq", [NH, Q], F32)
    biasEtm4 = sb("biasEtm4", [128, NCH * NH], F32)
    wtm4 = sb("wtm4", [128, NCH * NH], BF16)
    decN4 = sb("decN4", [128, NCH * NH], F32)
    drow = sb("drow", [1, NH], F32)
    meanc = sb("meanc", [128, 1], F32)
    nmean = sb("nmean", [128, 1], F32)
    sqs = sb("sqs", [128, 1], F32)
    varc = sb("varc", [128, 1], F32)
    rstdc = sb("rstdc", [128, 1], F32)
    rstd_t = sb("rstd_t", [1, 128], F32)
    rstd_c = sb("rstd_c", [128, 2], F32)
    tail = sb("tail", [128, NFT * 3], BF16)

    R1 = sb("R1", [128, 33792], BF16)
    R2 = sb("R2", [128, 32768], BF16)

    def carve(arena, aname, off_b, nelem, dt):
        if dt == BF16:
            ap = arena[:][:, off_b // 2: off_b // 2 + nelem]
        else:
            ap = arena[:].bitcast(F32)[:, off_b // 4: off_b // 4 + nelem]
        nbytes = nelem * (2 if dt == BF16 else 4)
        gran = 4096
        keys = [f"{aname}_{i}" for i in range(off_b // gran, (off_b + nbytes - 1) // gran + 1)]
        return ap, keys

    Dfm, kDfm = carve(R1, "R1", 0, 32768, BF16)
    zsfm, kzs = carve(R1, "R1", 0, 16384, BF16)
    accv, kacc = carve(R1, "R1", 32768, 17408, BF16)
    xnv, kxn = carve(R2, "R2", 0, 8192, BF16)
    ext, kext = carve(R2, "R2", 16384, NFT * (TT + 3), BF16)
    ctmp, kctmp = carve(R2, "R2", 53248, 4608, BF16)
    u_sub_t = sb("u_sub", [128, 2048], F32)
    u_sub, ku_s = u_sub_t[:], ["usub0", "usub1"]
    xn_tm, kxn_tm = carve(R2, "R2", 53248, 2048, BF16)
    gsil, kgsil = carve(R2, "R2", 16384, 8192, BF16)
    x_tm, kx_tm = carve(R2, "R2", 0, 4096, BF16)
    wxv, kwx = carve(R2, "R2", 8192, 4096, BF16)
    ygsq, kygsq = carve(R2, "R2", 8192, 4096, BF16)
    ygv, kyg = [], []
    for i in range(2):
        a, k = carve(R2, "R2", 16384 + i * 8192, 4096, BF16)
        ygv.append(a); kyg.append(k)
    EM, kEM = carve(R2, "R2", 32768, 2048, BF16)
    Gv, kGv = carve(R2, "R2", 36864, 2048, BF16)
    CBb, kCB = carve(R2, "R2", 40960, 128, BF16)
    Btm, kBtm = carve(R2, "R2", 41984, 128, BF16)
    outst = []; kout = []
    for i in range(2):
        a, k = carve(R2, "R2", 43008 + i * 8192, 2048, F32)
        outst.append(a); kout.append(k)
    prevb, kprevb = carve(R2, "R2", 59392, 1024, BF16)
    gtmr, kgtmr = carve(R2, "R2", 61440, 2048, BF16)
    u_rel, ku_rel = carve(R2, "R2", 16384, 2048, F32)

    pb = [es.enter_context(nc.psum_tensor(f"pb{i}", [128, 512], F32)) for i in range(6)]
    pt = [es.enter_context(nc.psum_tensor(f"pt{i}", [128, 1024], BF16)) for i in range(2)]

    def pk(bank, c0, c1):
        return [f"P{bank}_{i}" for i in range(c0 // 128, (c1 - 1) // 128 + 1)] + [
            f"PB{bank}"]

    sems = {}
    for e in ENGS:
        sems[e] = es.enter_context(nc.semaphore("s_" + e))
        sems[e + "d"] = es.enter_context(nc.semaphore("s_" + e + "d"))

    dbg_items = []

    def dump(name, ap, shape, dt, rkeys=()):
        if name in dbg:
            d = nc.dram_tensor("dbg_" + name, shape, dt, kind="ExternalOutput")
            dbg_items.append((name, d, ap, list(rkeys)))

    for key, dst, src in [("identb", identb, identb_d), ("identf", identf, identf_d),
                          ("maskrep", maskrep, maskrep_d),
                          ("cwp", cwp, cwp_d), ("cbp", cbp, cbp_d),
                          ("dp2", dp2, dp2_d), ("nwc", nwc, nw_d),
                          ("b0c", b0c, b0_d), ("b1c", b1c, b1_d),
                          ("dtb", dtb, dtb_d), ("alog", alog, alog_d)]:
        p.emit("gpsimd", lambda g, a=dst, b=src: g.dma_start(a[:], b[:]),
               writes=[key], dma=True)
    p.emit("vector", lambda e: e.memset(ones1f[:], 1.0), writes=["ones1f"])
    p.emit("vector", lambda e: e.memset(onescb[:], 1.0), writes=["onescb"])
    p.emit("vector", lambda e: e.memset(zq[:], 0.0), writes=["zq"])
    p.emit("vector", lambda e: e.memset(prevT[:], 0.0), writes=["prevT"])
    p.emit("vector", lambda e: e.memset(tail[:], 0.0), writes=["tail"])
    p.emit("scalar", lambda e: e.activation(acol[:], alog[:], AF.Exp),
           reads=["alog"], writes=["acol_e"])
    p.emit("vector", lambda e: e.tensor_scalar_mul(acol[:], acol[:], -1.0),
           reads=["acol_e"], writes=["acol"])

    w0v = wq_0t[:].rearrange("p (g h k w) -> p g h k w", g=16, h=2, k=8)
    winv = wq_int[:].rearrange("p (g h k w) -> p g h k w", g=17, h=2, k=8)
    w1v = wq_1t[:].rearrange("p (n h k w) -> p n h k w", n=16, h=2, k=32)
    wov = wq_ot[:].rearrange("p (k n w) -> p k n w", k=32, n=2)

    cast_srcs = [carve(R2, "R2", 32768, 2048, F32),
                 carve(R2, "R2", 40960, 2048, F32)]
    cast_dsts = [carve(R2, "R2", 49152, 2048, BF16),
                 carve(R2, "R2", 53248, 2048, BF16)]
    cast_ctr = [0]

    def cast_piece(qkey, src_ap, dst_ap, dshape=None, fold_col=None):
        i = cast_ctr[0]; cast_ctr[0] += 1
        pr = i % 2
        n = src_ap.shape[1]
        (sap, kcs2), (dap, kcd2) = cast_srcs[pr], cast_dsts[pr]
        s = sap[:, 0:n]
        d = dap[:, 0:n]
        dd = d if dshape is None else d.rearrange(*dshape[0], **dshape[1])
        q = "sync" if pr == 0 else "gpsimd"
        p.emit(q, lambda g, s=s, src_ap=src_ap: g.dma_start(s, src_ap),
               reads=[f"cd{pr}"] + kcd2 + kcs2, writes=[f"cs{pr}"] + kcs2,
               dma=True)
        if fold_col is not None:
            p.emit("vector", lambda e, s=s, d=d, fc=fold_col:
                   e.tensor_scalar_mul(d, s, fc),
                   reads=[f"cs{pr}", "nwc"] + kcs2 + kcd2,
                   writes=[f"cd{pr}"] + kcd2)
        else:
            p.emit("vector", lambda e, s=s, d=d: e.tensor_copy(d, s),
                   reads=[f"cs{pr}"] + kcs2 + kcd2, writes=[f"cd{pr}"] + kcd2)
        p.emit("scalar", lambda g, dst_ap=dst_ap, dd=dd: g.dma_start(
            dst_ap, dd), reads=[f"cd{pr}"] + kcd2, writes=[qkey], dma=True)

    pp_tasks = []

    def pump(n):
        for _ in range(min(n, len(pp_tasks))):
            pp_tasks.pop(0)()

    def _emit_prepass():
        for kg in range(64):
            kh, kk = kg // 32, kg % 32
            pp_tasks.append(lambda kg=kg, kh=kh, kk=kk: cast_piece(
                "wq1_ready", w1T_d[ts(kg), :],
                w1v[:, :, kh, kk, :], (("p (n w) -> p n w",), {"n": 16})))
        for r in range(32):
            pp_tasks.append(lambda r=r: cast_piece(
                "wqo_ready", woT_d[ts(r), :],
                wov[:, r, :, :], (("p (n w) -> p n w",), {"n": 2}),
                fold_col=nwc[:, r:r + 1]))


    def wbk(lo, hi):
        return [f"WB_{i}" for i in range(lo // 1024, (hi - 1) // 1024 + 1)]

    def load_wtile(view_slice, half, rkey, par):
        lo = par * 8192 + half * 4096
        dst = wbuf[:, lo: lo + 4096]
        q = "sync" if half == 0 else "gpsimd"
        p.emit(q, lambda g, dst=dst, s=view_slice: g.dma_start(dst, s),
               reads=[rkey], writes=wbk(lo, lo + 4096), dma=True)

    def wsl(half, kh, blockw, off, length, par):
        base = par * 8192 + half * 4096 + kh * blockw + off
        return wbuf[:, base: base + length], wbk(base, base + length)

    def direct_cast(wT_d, wview, grp, gpar, skey, eng, ncols=512, npad=0):
        col0 = grp * 512 if npad == 0 else 8192
        for r in range(16):
            half, kt = r // 8, r % 8
            pr = r % 2
            s = u_sub[:, pr * 512: pr * 512 + ncols]
            sk = [f"dcS{pr}"]
            base = gpar * 8192 + half * 4096 + kt * 512
            dst = wbuf[:, base: base + ncols]
            dk = wbk(base, base + 512)
            q = "sync" if pr == 0 else "gpsimd"
            p.emit(q, lambda g, s=s, src=wT_d[ts(r), col0: col0 + ncols]:
                   g.dma_start(s, src), reads=sk, writes=sk, dma=True)
            if eng == "vector":
                p.emit(eng, lambda e, s=s, d=dst: e.tensor_copy(d, s),
                       reads=sk + dk, writes=dk + ["dccp"])
            else:
                p.emit(eng, lambda e, s=s, d=dst: e.copy(d, s),
                       reads=sk + dk, writes=dk + ["dccp"])
        for half in range(2):
            lo = gpar * 8192 + half * 4096
            if npad:
                w8 = wbuf[:, lo: lo + 4096].rearrange("p (k w) -> p k w", k=8)
                p.emit("vector", lambda e, w8=w8, npad=npad: e.memset(
                    w8[:, :, 512 - npad: 512], 0.0),
                    reads=wbk(lo, lo + 4096), writes=wbk(lo, lo + 4096))
            p.emit("scalar", lambda g, d=wview[:, grp, half],
                   s=wbuf[:, lo: lo + 4096]: g.dma_start(d, s),
                   reads=wbk(lo, lo + 4096), writes=[skey], dma=True)

    tposed = [0]

    def transpose_b(src_ap, dst_ap, rkeys, wkeys):
        i = tposed[0]; tposed[0] += 1
        bk, sl = (i // 8) % 2, i % 8
        key = f"T{bk}_{sl}"
        outp = pt[bk][0: src_ap.shape[1], sl * 128: sl * 128 + src_ap.shape[0]]
        idn = identb[0:src_ap.shape[0], 0:src_ap.shape[0]]
        p.emit("tensor", lambda e, s=src_ap, d=outp, idn=idn: e.transpose(d, s, idn),
               reads=list(rkeys) + ["identb"], writes=[key, f"TB{bk}"])
        if sl % 2 == 0:
            p.emit("scalar", lambda e, s=outp, d=dst_ap: e.copy(d, s),
                   reads=[key, f"TB{bk}"], writes=list(wkeys))
        else:
            p.emit("vector", lambda e, s=outp, d=dst_ap: e.tensor_copy(d, s),
                   reads=[key, f"TB{bk}"], writes=list(wkeys))

    for tt in range(n_ttiles):
        seq_start = (tt % ntt_per_seq) == 0
        t0 = tt * TT

        for s in range(NSUB):
            p.emit("gpsimd", lambda g, s=s, t0=t0: g.dma_start(
                u_sub, u_d[t0 + s * 128: t0 + (s + 1) * 128, :]),
                reads=ku_s, writes=ku_s + ["u_sub"], dma=True)
            p.emit("vector", lambda e: e.reduce_sum(meanc[:], u_sub, axis=AX.X),
                   reads=ku_s + ["u_sub"], writes=["meanc_s"])
            p.emit("scalar", lambda e: e.activation(
                outst[0], u_sub, AF.Square, accum_out=sqs[:]),
                reads=ku_s + ["u_sub"] + kout[0], writes=kout[0] + ["sqs"])
            p.emit("vector", lambda e: e.tensor_scalar_mul(meanc[:], meanc[:], 1.0 / DM),
                   reads=["meanc_s"], writes=["meanc"])
            p.emit("vector", lambda e: e.tensor_scalar_mul(nmean[:], meanc[:], -1.0),
                   reads=["meanc"], writes=["nmean"])
            p.emit("vector", lambda e: e.tensor_scalar(
                varc[:], sqs[:], 1.0 / DM, EPS, ALU.mult, ALU.add),
                reads=["sqs"], writes=["varc_s"])
            p.emit("vector", lambda e: e.scalar_tensor_tensor(
                varc[:], meanc[:], nmean[:, 0:1], varc[:], ALU.mult, ALU.add),
                reads=["meanc", "nmean", "varc_s"], writes=["varc"])
            p.emit("scalar", lambda e: e.activation(varc[:], varc[:], AF.Sqrt),
                   reads=["varc"], writes=["varc_q"])
            p.emit("vector", lambda e: e.reciprocal(rstdc[:], varc[:]),
                   reads=["varc_q"], writes=["rstdc"])
            p.emit("vector", lambda e: e.tensor_scalar(
                xn_tm, u_sub, meanc[:], rstdc[:], ALU.subtract, ALU.mult),
                reads=ku_s + ["u_sub", "meanc", "rstdc"] + kxn_tm,
                writes=kxn_tm + ["xn_tm"])
            for kt in range(16):
                dst = xnv[:, kt * TT + s * 128: kt * TT + (s + 1) * 128]
                transpose_b(xn_tm[:, ts(kt)], dst, kxn_tm + ["xn_tm"],
                            kxn + ["xnv"])
        dump("xnfm", xnv, [128, 16 * TT], BF16, ["xnv"] + kxn)

        if tt == 0:
            _emit_prepass()

        if tt == 0:
            p.emit("vector", lambda e: e.engine_nop(),
                   writes=["u_sub", "usub0", "usub1", "dcS0", "dcS1"])
        for grp in range(16):
            gpar = grp % 2
            if tt == 0:
                direct_cast(w0T_d, w0v, grp, gpar, "wq0_ready", "vector")
            else:
                for half in range(2):
                    load_wtile(w0v[:, grp, half], half, "wq0_ready", gpar)
            for ti in range(4):
                dt_i = grp * 4 + ti
                bank = dt_i % 3
                bkeys = pk(bank, 0, 512)
                for k in range(16):
                    half, kh = divmod(k, 8)
                    l, lk = wsl(half, kh, 512, ti * 128, 128, gpar)
                    p.emit("tensor", lambda e, l=l, k=k,
                           b=bank: e.matmul(pb[b][:], l, xnv[:, ts(k, TT)],
                                            start=(k == 0), stop=(k == 15)),
                           reads=lk + ["xnv"] + kxn, writes=bkeys,
                           inc=(k == 15))
                ddst = Dfm[:, dt_i * TT: (dt_i + 1) * TT]
                p.emit("scalar", lambda e, d=ddst, b=bank, dt_i=dt_i: e.activation(
                    d, pb[b][:], AF.Gelu, bias=b0c[:, dt_i:dt_i + 1]),
                    reads=bkeys + ["b0c"], writes=kDfm + ["Dfm"])
            if tt == 0:
                pump(4)
        dump("Dfm", Dfm, [128, 64 * TT], BF16, ["Dfm"] + kDfm)

        for nt in range(16):
            bank = nt % 3
            bkeys = pk(bank, 0, 512)
            npar = nt % 2
            for kh in range(2):
                load_wtile(w1v[:, nt, kh], kh, "wq1_ready", npar)
                for k in range(32):
                    kg = kh * 32 + k
                    l, lk = wsl(kh, k, 128, 0, 128, npar)
                    p.emit("tensor", lambda e, l=l, kg=kg,
                           b=bank: e.matmul(pb[b][:], l, Dfm[:, ts(kg, TT)],
                                            start=(kg == 0), stop=(kg == 63)),
                           reads=lk + ["Dfm"] + kDfm, writes=bkeys,
                           inc=(kg == 63))
            gdst = gsil[:, nt * TT: (nt + 1) * TT]
            p.emit("scalar", lambda e, d=gdst, b=bank, nt=nt: e.activation(
                d, pb[b][:], AF.Silu, bias=b1c[:, nt:nt + 1]),
                reads=bkeys + ["b1c"], writes=kgsil + ["gsil"])
            if tt == 0:
                pump(2)
        for s in range(NSUB):
            for nt in range(16):
                transpose_b(gsil[:, nt * TT + s * 128: nt * TT + (s + 1) * 128],
                            gtmr[:, ts(nt)], kgsil + ["gsil"], kgtmr + ["gtmr"])
            p.emit("scalar", lambda g, s=s, tt=tt: g.dma_start(
                gtm_q[tt, s * 128:(s + 1) * 128, :], gtmr),
                reads=kgtmr + ["gtmr"], writes=[f"gq{tt}_{s}"], dma=True)
        dump("gsil", gsil, [128, 16 * TT], BF16, ["gsil"] + kgsil)

        ext3_ = ext.rearrange("p (ft c) -> p ft c", ft=NFT)
        acc3_ = accv.rearrange("p (ft c) -> p ft c", ft=NFT)

        def extk(ft0, gsz):
            lo = 16384 + ft0 * (TT + 3) * 2
            hi = 16384 + (ft0 + gsz) * (TT + 3) * 2
            return [f"R2_{i}" for i in range(lo // 4096, (hi - 1) // 4096 + 1)]

        def acck(ft0, gsz):
            lo = 32768 + ft0 * TT * 2
            hi = 32768 + (ft0 + gsz) * TT * 2
            return [f"R1_{i}" for i in range(lo // 4096, (hi - 1) // 4096 + 1)]

        def conv_group(ft0, gsz, seq_start=seq_start):
            ek = extk(ft0, gsz)
            ak = acck(ft0, gsz)
            if seq_start:
                p.emit("vector", lambda e, ft0=ft0, gsz=gsz: e.memset(
                    ext3_[:, ft0:ft0 + gsz, 0:3], 0.0),
                    reads=ek, writes=ek + ["ext_h"])
            else:
                p.emit("vector", lambda e, ft0=ft0, gsz=gsz: e.tensor_copy(
                    ext3_[:, ft0:ft0 + gsz, 0:3],
                    tail[:].rearrange("p (f c) -> p f c", f=NFT)[:, ft0:ft0 + gsz, :]),
                    reads=["tail"] + ek, writes=ek + ["ext_h"])
            for w in range(4):
                s = ext3_[:, ft0: ft0 + gsz, w: w + TT]
                cw = cwp[:, ft0 * 4 + w: (ft0 + gsz - 1) * 4 + w + 1: 4]
                cwb = cw.unsqueeze(2).broadcast_to([128, gsz, TT])
                ac = acc3_[:, ft0: ft0 + gsz, :]
                if w == 0:
                    p.emit("vector", lambda e, s=s, c=cwb, a=ac: e.tensor_tensor(
                        a, s, c, ALU.mult),
                        reads=ek + ["ext_h", "cwp"] + ak,
                        writes=ak + ["acc_p"])
                else:
                    t3 = ctmp[:, 0: gsz * TT].rearrange("p (f c) -> p f c", f=gsz)
                    p.emit("vector", lambda e, s=s, c=cwb, t=t3: e.tensor_tensor(
                        t, s, c, ALU.mult),
                        reads=ek + ["ext_h", "cwp", "gefence"] + kctmp,
                        writes=kctmp + ["ctmp"])
                    p.emit("vector", lambda e, t=t3, a=ac: e.tensor_tensor(
                        a, a, t, ALU.add),
                        reads=kctmp + ["ctmp", "acc_p"] + ak,
                        writes=ak + ["acc_p"])
            for ft in range(ft0, ft0 + gsz):
                a = accv[:, ft * TT: (ft + 1) * TT]
                p.emit("scalar", lambda e, a=a, ft=ft: e.activation(
                    a, a, AF.Silu, bias=cbp[:, ft:ft + 1]),
                    reads=["acc_p", "cbp"] + ak, writes=ak + ["accv"])

        def _emit_dt_hoist():
            p.emit("vector", lambda e: e.tensor_scalar_add(
                dtraw[:], dtraw[:], dtb[:, 0:1]),
                reads=["dtraw", "dtb"], writes=["dtraw"])
            p.emit("scalar", lambda e: e.activation(dtv4[:], dtraw[:], AF.Abs),
                   reads=["dtraw"], writes=["dt_abs"])
            p.emit("scalar", lambda e: e.activation(dtv4[:], dtv4[:], AF.Exp, scale=-1.0),
                   reads=["dt_abs"], writes=["dt_exp"])
            p.emit("vector", lambda e: e.tensor_scalar_add(dtv4[:], dtv4[:], 1.0),
                   reads=["dt_exp"], writes=["dt_e1"])
            p.emit("scalar", lambda e: e.activation(dtv4[:], dtv4[:], AF.Ln),
                   reads=["dt_e1"], writes=["dt_l1p"])
            p.emit("vector", lambda e: e.tensor_scalar_max(dacs4[:], dtraw[:], 0.0),
                   reads=["dtraw"], writes=["dt_mx"])
            p.emit("vector", lambda e: e.tensor_add(dtv4[:], dtv4[:], dacs4[:]),
                   reads=["dt_mx", "dt_l1p"], writes=["dtv4"])
            p.emit("vector", lambda e: e.tensor_scalar_mul(
                dtraw[:], dtv4[:], acol[:, 0:1]),
                reads=["dtv4", "acol", "dtraw"], writes=["dtA4"])
            for c in range(NCH):
                cg = tt * NCH + c
                tc = c * Q
                p.emit("vector", lambda e, tc=tc: e.tensor_tensor_scan(
                    dacs4[:, tc:tc + Q], dtraw[:, tc:tc + Q], zq[:], 0.0,
                    ALU.add, ALU.add),
                    reads=["dtA4", "zq", "dt_mx"], writes=["dacs4"])
                p.emit("gpsimd", lambda g, cg=cg, tc=tc: g.dma_start(
                    dab[cg], dacs4[:, tc:tc + Q]),
                    reads=["dacs4"], writes=[f"dab{cg}"], dma=True)
            p.emit("scalar", lambda e: e.activation(lndt4[:], dtv4[:], AF.Ln),
                   reads=["dtv4"], writes=["lndt4"])
            p.emit("vector", lambda e: e.tensor_sub(lndt4[:], lndt4[:], dacs4[:]),
                   reads=["lndt4", "dacs4"], writes=["biasE4"])
            for c in range(NCH):
                tc = c * Q
                p.emit("scalar", lambda e, tc=tc: e.activation(
                    wfm4[:, tc:tc + Q], dacs4[:, tc:tc + Q], AF.Exp,
                    bias=dacs4[:, tc + Q - 1:tc + Q], scale=-1.0),
                    reads=["dacs4"], writes=["wfm4e"])
            p.emit("vector", lambda e: e.tensor_mul(wfm4[:], wfm4[:], dtv4[:]),
                   reads=["wfm4e", "dtv4"], writes=["wfm4"])
            p.emit("vector", lambda e: e.tensor_copy(wfb4[:], wfm4[:]),
                   reads=["wfm4"], writes=["wfb4"])
            for c in range(NCH):
                tc = c * Q
                p.emit("tensor", lambda e, tc=tc: e.transpose(
                    pb[5][:, 0:64], lndt4[:, tc:tc + Q], identf[0:64, 0:64]),
                    reads=["biasE4", "identf"], writes=pk(5, 0, 128))
                p.emit("vector", lambda e, c=c: e.tensor_copy(
                    biasEtm4[:, c * NH:(c + 1) * NH], pb[5][:, 0:64]),
                    reads=pk(5, 0, 128), writes=["biasEtm4"])
                transpose_b(wfb4[:, tc:tc + Q], wtm4[:, c * NH:(c + 1) * NH],
                            ["wfb4"], ["wtm4"])
                p.emit("tensor", lambda e, tc=tc: e.matmul(
                    pb[5][0:1, 128:192], dacs4[:, tc + Q - 1:tc + Q],
                    identf[0:64, 0:64], start=True, stop=True),
                    reads=["dacs4", "identf"], writes=pk(5, 128, 192))
                p.emit("vector", lambda e: e.tensor_copy(drow[:], pb[5][0:1, 128:192]),
                       reads=pk(5, 128, 192), writes=["drow"])
                p.emit("tensor", lambda e: e.matmul(
                    pb[5][:, 192:256], ones1f[:], drow[:], start=True, stop=True),
                    reads=["drow", "ones1f"], writes=pk(5, 192, 256))
                p.emit("scalar", lambda e, c=c: e.activation(
                    decN4[:, c * NH:(c + 1) * NH], pb[5][:, 192:256], AF.Exp),
                    reads=pk(5, 192, 256), writes=["decN4"])


        conv_sched = {10: (0, 9), 12: (9, 9), 14: (18, 8), 15: (26, 8)}
        p.emit("vector", lambda e: e.engine_nop(),
               reads=["gv0", "gv1", "em0", "em1", "pvb0", "pvb1",
                      "bcast0", "bcast1", "EM", "Gv"],
               writes=["gefence"])
        if tt == 0:
            pump(len(pp_tasks))
        for grp in [16] + list(range(16)):
            ntile = 4 if grp < 16 else 3
            gpar = grp % 2
            if tt == 0:
                direct_cast(winT_d, winv, grp, gpar, "wqi_ready", "scalar",
                            ncols=512 if grp < 16 else 320,
                            npad=0 if grp < 16 else 192)
            else:
                for half in range(2):
                    load_wtile(winv[:, grp, half], half, "wqi_ready", gpar)
            for ti in range(ntile):
                dt_i = grp * 4 + ti
                if dt_i == 66:
                    bkeys = pk(4, 0, 512)
                    for k in range(16):
                        half, kh = divmod(k, 8)
                        l, lk = wsl(half, kh, 512, ti * 128, 64, gpar)
                        p.emit("tensor", lambda e, l=l, k=k: e.matmul(
                            pb[4][0:64, :], l, xnv[:, ts(k, TT)],
                            start=(k == 0), stop=(k == 15)),
                            reads=lk + ["xnv"] + kxn, writes=bkeys,
                            inc=(k == 15))
                    p.emit("vector", lambda e: e.tensor_copy(
                        dtraw[:], pb[4][0:64, :]),
                        reads=bkeys, writes=["dtraw"])
                    _emit_dt_hoist()
                    continue
                bank = dt_i % 3
                bkeys = pk(bank, 0, 512)
                for k in range(16):
                    half, kh = divmod(k, 8)
                    l, lk = wsl(half, kh, 512, ti * 128, 128, gpar)
                    p.emit("tensor", lambda e, l=l, k=k,
                           b=bank: e.matmul(pb[b][:], l, xnv[:, ts(k, TT)],
                                            start=(k == 0), stop=(k == 15)),
                           reads=lk + ["xnv"] + kxn, writes=bkeys,
                           inc=(k == 15))
                if dt_i < 32:
                    zdst = zsfm[:, dt_i * TT: (dt_i + 1) * TT]
                    p.emit("scalar", lambda e, d=zdst, b=bank: e.activation(
                        d, pb[b][:], AF.Silu), reads=bkeys,
                        writes=kzs + ["zsfm"])
                else:
                    ft = dt_i - 32
                    edst = ext[:, ft * (TT + 3) + 3: ft * (TT + 3) + 3 + TT]
                    p.emit("vector", lambda e, d=edst, b=bank: e.tensor_copy(
                        d, pb[b][:]), reads=bkeys + ["gefence"],
                        writes=extk(ft, 1) + ["ext"])
            if grp in conv_sched:
                conv_group(*conv_sched[grp])
        if tt == 0:
            p.emit("vector", lambda e: e.engine_nop(),
                   reads=["dccp"],
                   writes=["u_sub", "usub0", "usub1"])
        if "xne" in dbg and tt == 0:
            xned = nc.dram_tensor("dbg_xne", [128, 16 * TT], BF16,
                                  kind="ExternalOutput")
            p.emit("gpsimd", lambda g: g.dma_start(xned[:], xnv),
                   reads=["xnv"] + kxn, writes=["xnedump"], dma=True)
        dump("zsfm", zsfm, [128, 32 * TT], BF16, ["zsfm"] + kzs)
        dump("dtraw", dtraw[:], [NH, TT], F32, ["dtraw"])

        ext3 = ext.rearrange("p (ft c) -> p ft c", ft=NFT)
        p.emit("vector", lambda e, ext3=ext3: e.tensor_copy(
            tail[:].rearrange("p (ft c) -> p ft c", ft=NFT), ext3[:, :, TT: TT + 3]),
            reads=kext + ["ext", "ext_h"], writes=["tail"])
        dump("accv", accv, [128, NFT * TT], BF16, ["accv"] + kacc)

        p.emit("vector", lambda e: e.engine_nop(),
               reads=kGv + kEM + kprevb + ["ext", "accv"],
               writes=["ssdfence"])
        for c in range(NCH):
            cg = tt * NCH + c
            tc = c * Q
            par = c % 2
            def wo_load(kq):
                nh2q, kk2 = divmod(kq, 32)
                slot = kq % 16
                wosl_ = wbuf[:, slot * 1024: slot * 1024 + 1024]
                wok_ = wbk(slot * 1024, slot * 1024 + 1024)
                p.emit("sync" if kq % 2 == 0 else "scalar",
                       lambda g, w=wosl_, kk2=kk2, nh2q=nh2q: g.dma_start(
                           w, wov[:, kk2, nh2q]),
                       reads=["wqo_ready"], writes=wok_, dma=True)

            if par == 0:
                for kq in range(16):
                    wo_load(kq)
            Bfm = accv[:, 32 * TT + tc: 32 * TT + tc + Q]
            Cfm = accv[:, 33 * TT + tc: 33 * TT + tc + Q]
            transpose_b(Bfm, Btm, kacc + ["accv"], kBtm + ["Btm"])
            for ftx in range(32):
                transpose_b(accv[:, ftx * TT + tc: ftx * TT + tc + Q],
                            x_tm[:, ts(ftx)], kacc + ["accv"], kx_tm + ["x_tm"])
            wtm_b = wtm4[:, c * NH:(c + 1) * NH].unsqueeze(2).broadcast_to(
                [128, NH, DH])
            p.emit("vector", lambda e, wb=wtm_b: e.tensor_tensor(
                wxv.rearrange("p (h d) -> p h d", h=NH),
                x_tm.rearrange("p (h d) -> p h d", h=NH), wb, ALU.mult),
                reads=kx_tm + ["x_tm", "wtm4"] + kwx, writes=kwx + ["wxv"])
            p.emit("tensor", lambda e, B=Bfm, C=Cfm: e.matmul(
                pb[4][:, 0:128], B, C, start=True, stop=True),
                reads=kacc + ["accv"], writes=pk(4, 0, 128))
            p.emit("vector", lambda e: e.tensor_copy(CBb, pb[4][:, 0:128]),
                   reads=pk(4, 0, 128), writes=kCB + ["CBb"])

            def pre_dma(s8):
                sp = s8 % 2
                uhalf = u_sub[:, sp * 1024:(sp + 1) * 1024]
                kb = f"usub{sp}"
                bsrc = dab[cg].rearrange("h q -> (h q)").unsqueeze(0)[
                    0:1, s8 * 1024: (s8 + 1) * 1024].partition_broadcast(128)
                p.emit("gpsimd", lambda g, bsrc=bsrc, uhalf=uhalf: g.dma_start(
                    uhalf, bsrc),
                    reads=[f"dab{cg}", kb], writes=[kb, f"bcast{sp}"], dma=True)

            def pre_exp(s8):
                sp = s8 % 2
                uhalf = u_sub[:, sp * 1024:(sp + 1) * 1024]
                gvh = Gv[:, sp * 1024:(sp + 1) * 1024]
                kb = f"usub{sp}"; kg_ = f"gv{sp}"
                p.emit("scalar", lambda e, gvh=gvh, uhalf=uhalf: e.activation(
                    gvh, uhalf, AF.Exp),
                    reads=[kb, f"bcast{sp}", "ssdfence"],
                    writes=[kg_, kg_ + "g"])

            def ssd_front(s8):
                sp = s8 % 2
                h0 = s8 * 8
                uhalf = u_sub[:, sp * 1024:(sp + 1) * 1024]
                gvh = Gv[:, sp * 1024:(sp + 1) * 1024]
                emh = EM[:, sp * 1024:(sp + 1) * 1024]
                pvh = prevb[:, sp * 512:(sp + 1) * 512]
                kb = f"usub{sp}"; kg_ = f"gv{sp}"; ke_ = f"em{sp}"
                kp_ = f"pvb{sp}"
                u2v = uhalf.rearrange("p (o w) -> p o w", o=2)
                mrep = maskrep[:].unsqueeze(1).broadcast_to([128, 2, 512])
                p.emit("vector", lambda e, u2v=u2v, mrep=mrep: e.tensor_tensor(
                    u2v, u2v, mrep, ALU.add),
                    reads=["maskrep", kg_, f"bcast{sp}", kb],
                    writes=[kb, f"bcastm{sp}"])
                u8v = uhalf.rearrange("p (h w) -> p h w", h=8)
                bsl = biasEtm4[:, c * NH + h0: c * NH + h0 + 8]
                bslb = bsl.unsqueeze(2).broadcast_to([128, 8, Q])
                p.emit("vector", lambda e, u8v=u8v, bslb=bslb: e.tensor_tensor(
                    u8v, u8v, bslb, ALU.add),
                    reads=["biasEtm4", f"bcastm{sp}", kb],
                    writes=[kb, f"bcastb{sp}"])
                p.emit("scalar", lambda e, emh=emh, uhalf=uhalf: e.activation(
                    emh, uhalf, AF.Exp),
                    reads=[kb, f"bcastb{sp}", "ssdfence"],
                    writes=[ke_ + "e", ke_])
                cb = Cfm.unsqueeze(1).broadcast_to([128, 8, Q])
                gv3 = gvh.rearrange("p (h q) -> p h q", h=8)
                p.emit("vector", lambda e, gv3=gv3, cb=cb: e.tensor_tensor(
                    gv3, gv3, cb, ALU.mult),
                    reads=[kg_, "accv"] + kacc, writes=[kg_ + "g"])
                cbb = CBb.unsqueeze(1).broadcast_to([128, 8, Q])
                em3 = emh.rearrange("p (h q) -> p h q", h=8)
                p.emit("vector", lambda e, em3=em3, cbb=cbb: e.tensor_tensor(
                    em3, em3, cbb, ALU.mult),
                    reads=[ke_ + "e", "CBb"] + kCB, writes=[ke_])
                p.emit("scalar", lambda e, pvh=pvh, h0=h0: e.copy(
                    pvh, prevT[:, h0 * DH: (h0 + 8) * DH]),
                    reads=["prevT", kp_], writes=[kp_])

            def ssd_back(s8):
                sp = s8 % 2
                h0 = s8 * 8
                gvh = Gv[:, sp * 1024:(sp + 1) * 1024]
                emh = EM[:, sp * 1024:(sp + 1) * 1024]
                pvh = prevb[:, sp * 512:(sp + 1) * 512]
                kg_ = f"gv{sp}"; ke_ = f"em{sp}"; kp_ = f"pvb{sp}"
                ybank = 4 + sp
                for hh in range(8):
                    h = h0 + hh
                    pair, pr = divmod(hh, 2)
                    off = pair * 128
                    yp = pb[ybank][pr * 64:(pr + 1) * 64, off:off + 128]
                    ykeys = pk(ybank, off, off + 128)
                    p.emit("tensor", lambda e, yp=yp, hh=hh, pvh=pvh, gvh=gvh:
                        e.matmul(yp, pvh[:, ts(hh, DH)], gvh[:, ts(hh)],
                                 start=True, stop=False),
                        reads=[kp_, kg_ + "g"], writes=ykeys, inc=False)
                    p.emit("tensor", lambda e, yp=yp, h=h, hh=hh, emh=emh:
                        e.matmul(yp, x_tm[:, ts(h, DH)], emh[:, ts(hh)],
                                 start=False, stop=True),
                        reads=kx_tm + ["x_tm", ke_], writes=ykeys)
                for pair in range(4):
                    gpair = s8 * 4 + pair
                    off = pair * 128
                    yp = pb[ybank][:, off:off + 128]
                    xsl = accv[:, gpair * TT + tc: gpair * TT + tc + Q]
                    p.emit("vector", lambda e, yp=yp, xsl=xsl, gpair=gpair,
                           par=par:
                        e.scalar_tensor_tensor(
                            ygv[par][:, ts(gpair)], xsl, dp2[:, gpair:gpair + 1],
                            yp, ALU.mult, ALU.add),
                        reads=pk(ybank, off, off + 128) + kacc +
                        ["accv", "dp2"] + kyg[par],
                        writes=kyg[par] + [f"yg{par}"])
                p.emit("tensor", lambda e, sp=sp, s8=s8: e.matmul(
                    pb[sp][:], Btm, wxv[:, s8 * 512: (s8 + 1) * 512],
                    start=True, stop=True),
                    reads=kBtm + ["Btm", "wxv"] + kwx, writes=pk(sp, 0, 512))
                dsl = decN4[:, c * NH + h0: c * NH + h0 + 8]
                db = dsl.unsqueeze(2).broadcast_to([128, 8, DH])
                pv = prevT[:, h0 * DH: (h0 + 8) * DH]
                pv3 = pv.rearrange("p (h d) -> p h d", h=8)
                p.emit("gpsimd", lambda e, pv3=pv3, db=db: e.tensor_tensor(
                    pv3, pv3, db, ALU.mult),
                    reads=["prevT", "decN4", kp_], writes=["prevT_s"])
                p.emit("vector", lambda e, pv=pv, sp=sp: e.tensor_tensor(
                    pv, pv, pb[sp][:, 0:512], ALU.add),
                    reads=["prevT_s"] + pk(sp, 0, 512), writes=["prevT"])


            pre_dma(0)
            pre_exp(0)
            for s8 in range(9):
                if s8 + 1 < 8:
                    pre_dma(s8 + 1)
                if s8 < 8:
                    ssd_front(s8)
                if s8 >= 1:
                    ssd_back(s8 - 1)
                if s8 + 1 < 8:
                    pre_exp(s8 + 1)

            if "y1" in dbg:
                p.emit("gpsimd", lambda g, cg=cg, par=par: g.dma_start(
                    y1p_d[cg], ygv[par]), reads=kyg[par] + [f"yg{par}"],
                    writes=["y1pdbg"], dma=True)
            zsl = zsfm.rearrange("p (f c) -> p f c", f=32)[:, :, tc:tc + Q]
            y3 = ygv[par].rearrange("p (f c) -> p f c", f=32)
            p.emit("vector", lambda e, zsl=zsl, y3=y3: e.tensor_tensor(
                y3, y3, zsl, ALU.mult),
                reads=[f"yg{par}", "zsfm"] + kzs + kyg[par],
                writes=kyg[par] + [f"yg{par}"])
            p.emit("scalar", lambda e, par=par: e.activation(
                ygsq, ygv[par], AF.Square),
                reads=[f"yg{par}"] + kyg[par] + kygsq, writes=kygsq + ["ygsq"])
            for ftx in range(32):
                p.emit("tensor", lambda e, ftx=ftx: e.matmul(
                    pb[4][0:1, 256:384], onescb[:], ygsq[:, ts(ftx)],
                    start=(ftx == 0), stop=(ftx == 31)),
                    reads=kygsq + ["ygsq", "onescb"], writes=pk(4, 256, 384),
                    inc=(ftx == 31))
            p.emit("vector", lambda e: e.tensor_scalar(
                rstd_t[:], pb[4][0:1, 256:384], 1.0 / DI, EPS, ALU.mult, ALU.add),
                reads=pk(4, 256, 384), writes=["rstd_ts"])
            p.emit("scalar", lambda e: e.activation(
                rstd_t[:], rstd_t[:], AF.Ln),
                reads=["rstd_ts"], writes=["rstd_ln"])
            p.emit("scalar", lambda e: e.activation(
                rstd_t[:], rstd_t[:], AF.Exp, scale=-0.5),
                reads=["rstd_ln"], writes=["rstd_row"])
            p.emit("tensor", lambda e: e.matmul(
                pb[4][:, 384:385], rstd_t[:], ones1f[0:1, 0:1],
                start=True, stop=True),
                reads=["rstd_row", "ones1f"], writes=pk(4, 384, 512))
            p.emit("vector", lambda e, par=par: e.tensor_copy(
                rstd_c[:, par:par + 1], pb[4][:, 384:385]),
                reads=pk(4, 384, 512), writes=[f"rstd{par}"])

            if par == 1:
                for nh2 in range(2):
                    for k in range(32):
                        slot = (nh2 * 32 + k) % 16
                        wosl = wbuf[:, slot * 1024: slot * 1024 + 1024]
                        wok = wbk(slot * 1024, slot * 1024 + 1024)
                        for cc in range(2):
                            for sg in range(2):
                                bank = cc * 2 + sg
                                p.emit("tensor", lambda e, k=k, w=wosl, cc=cc,
                                       sg=sg, bank=bank: e.matmul(
                                    pb[bank][:], ygv[cc][:, ts(k)],
                                    w[:, ts(sg, 512)],
                                    start=(k == 0), stop=(k == 31)),
                                    reads=wok + [f"yg{cc}"] + kyg[cc],
                                    writes=pk(bank, 0, 512),
                                    inc=(cc == 1 and sg == 1))
                        if nh2 * 32 + k + 16 < 64:
                            wo_load(nh2 * 32 + k + 16)
                    for cc in range(2):
                        for sg in range(2):
                            bank = cc * 2 + sg
                            dsl = outst[cc][:, nh2 * 1024 + sg * 512:
                                            nh2 * 1024 + (sg + 1) * 512]
                            p.emit("vector", lambda e, d=dsl, bank=bank, cc=cc:
                                e.tensor_scalar_mul(d, pb[bank][:],
                                                    rstd_c[:, cc:cc + 1]),
                                reads=pk(bank, 0, 512) + [f"rstd{cc}"] + kout[cc],
                                writes=kout[cc] + [f"outst{cc}"])
                for cc in range(2):
                    trow = t0 + (c - 1 + cc) * Q
                    p.emit("scalar", lambda g, tt=tt, cc=cc, c=c: g.dma_start(
                        gtmr, gtm_q[tt, (c - 1 + cc) * Q: (c + cc) * Q, :]),
                        reads=[f"gq{tt}_{c - 1 + cc}"] + kgtmr,
                        writes=kgtmr + ["gtmr"], dma=True)
                    p.emit("scalar", lambda g, trow=trow: g.dma_start(
                        u_rel, u_d[trow: trow + Q, :]),
                        reads=ku_rel, writes=ku_rel + ["u_rel"], dma=True)
                    p.emit("vector", lambda e, cc=cc: e.tensor_tensor(
                        outst[cc], outst[cc], gtmr, ALU.mult),
                        reads=[f"outst{cc}", "gtmr"] + kgtmr + kout[cc],
                        writes=kout[cc] + [f"outst{cc}"])
                    p.emit("vector", lambda e, cc=cc: e.tensor_tensor(
                        outst[cc], outst[cc], u_rel, ALU.add),
                        reads=[f"outst{cc}", "u_rel"] + ku_rel + kout[cc],
                        writes=kout[cc] + [f"outst{cc}"])
                    p.emit("gpsimd", lambda g, trow=trow, cc=cc: g.dma_start(
                        out_d[trow: trow + Q, :], outst[cc]),
                        reads=[f"outst{cc}"] + kout[cc], writes=[f"od{trow}"],
                        dma=True)

    for name, d, ap, rkeys in dbg_items:
        p.emit("gpsimd", lambda g, d=d, ap=ap: g.dma_start(d[:], ap),
               reads=rkeys, writes=[f"dbg{name}"], dma=True)

    tail_waits = {"gpsimd": [("gpsimdd", p.counts["gpsimdd"]),
                             ("syncd", p.counts["syncd"]),
                             ("scalard", p.counts["scalard"])]}
    with nc.Block() as block:
        p.realize(block, sems, tail_waits)
    es.close()
    return nc


def _host_inputs(inputs, core, n_ttiles=NSEQ * NTT_PER_SEQ):
    u = inputs["L0_attn_u"]
    W_in = inputs["L0_attn_W_in"]
    conv_w = inputs["L0_attn_conv_w"]
    conv_b = inputs["L0_attn_conv_b"]
    dt_bias = inputs["L0_attn_dt_bias"]
    A_log = inputs["L0_attn_A_log"]
    Dp = inputs["L0_attn_Dp"]
    norm_w = inputs["L0_attn_norm_w"]
    W_out = inputs["L0_attn_W_out"]
    W0 = inputs["L0_ffn_W0"]
    b0 = inputs["L0_ffn_bias0"]
    W1 = inputs["L0_ffn_W1"]
    b1 = inputs["L0_ffn_bias1"]

    dp2 = np.empty((128, 32), np.float32)
    for g in range(32):
        dp2[:64, g] = Dp[2 * g]
        dp2[64:, g] = Dp[2 * g + 1]
    d = {
        "u": np.ascontiguousarray(u[2 * core: 2 * core + 2].reshape(NSEQ * SEQ, DM)[
            : n_ttiles * TT]),
        "winT": np.ascontiguousarray(W_in.T),
        "w0T": np.ascontiguousarray(W0.T),
        "w1T": np.ascontiguousarray(W1.T),
        "woT": np.ascontiguousarray(W_out.T),
        "cwp": np.ascontiguousarray(
            conv_w.reshape(NFT, 128, 4).transpose(1, 0, 2).reshape(128, NFT * 4)),
        "cbp": np.ascontiguousarray(conv_b.reshape(NFT, 128).T),
        "dtb": np.ascontiguousarray(dt_bias.reshape(NH, 1)),
        "alog": np.ascontiguousarray(A_log.reshape(NH, 1)),
        "dp2": dp2,
        "nw": np.ascontiguousarray(norm_w.reshape(32, 128).T),
        "b0c": np.ascontiguousarray(b0.reshape(64, 128).T),
        "b1c": np.ascontiguousarray(b1.reshape(16, 128).T),
        "identb": np.eye(128, dtype=BFD),
        "identf": np.eye(128, dtype=np.float32),
        "maskrep": np.tile(np.tril(np.full((128, 128), -1e38, np.float32), -1),
                           (1, 4)),
    }
    return d


def kernel(**inputs):
    nc = build()
    in_maps = [_host_inputs(inputs, c) for c in range(8)]
    res = run_bass_kernel_spmd(nc, in_maps, core_ids=list(range(8)))
    out = np.stack([r["out"].reshape(NSEQ, SEQ, DM) for r in res.results])
    return np.ascontiguousarray(out.reshape(16, SEQ, DM)).astype(np.float32)


# version 17 (final)
# speedup vs baseline: 1.0611x; 1.0546x over previous
import numpy as np
import ml_dtypes
from contextlib import ExitStack
import concourse.bass as bass
import concourse.mybir as mybir
from concourse.bass_utils import run_bass_kernel_spmd

F32 = mybir.dt.float32
BF16 = mybir.dt.bfloat16
AF = mybir.ActivationFunctionType
ALU = mybir.AluOpType
AX = mybir.AxisListType
BFD = ml_dtypes.bfloat16

DM = 2048
DIN = 8512
DI = 4096
DS = 128
NH = 64
DH = 64
CD = 4352
NFT = 34
Q = 128
TT = 512
NSUB = TT // 128
NCH = TT // Q
SEQ = 2048
NSEQ = 2
NTT_PER_SEQ = SEQ // TT
EPS = 1e-5

ENGS = ("tensor", "vector", "scalar", "gpsimd", "sync")


class Prog:

    DMA_WINDOW = 8

    def __init__(self):
        self.sched = {e: [] for e in ENGS}
        self.counts = {}
        for e in ENGS:
            self.counts[e] = 0
            self.counts[e + "d"] = 0
        self.bufs = {}
        self.water = {e: {} for e in ENGS}

    LOOPVARS = frozenset("""par c tc cg tt g4 grp ti k kb half nt s kt ft ft0
        gsz w dt_i bank pair pr hh h sg cc nh2 slot off gpair kg trow gw ntile
        seq_start t0 c0 cw_ r i bk sl""".split())

    def emit(self, eng, fn, reads=(), writes=(), dma=False, inc=True):
        if fn is not None:
            bad = set(fn.__code__.co_freevars) & self.LOOPVARS
            if bad:
                raise RuntimeError(f"lambda captures loop vars {bad} by closure"
                                   f" — bind them as default args")
        sem = eng + "d" if dma else eng
        amt = (16 if dma else 1) if inc else 0
        needs = {}

        def need(psem, pcnt):
            if psem == sem == "tensor":
                return
            if psem.endswith("d"):
                pcnt = self.counts[psem]
            if needs.get(psem, -1) < pcnt:
                needs[psem] = pcnt

        for k in reads:
            b = self.bufs.get(k)
            if b and b["w"] is not None:
                need(*b["w"])
        for k in writes:
            b = self.bufs.get(k)
            if b:
                if b["w"] is not None:
                    need(*b["w"])
                for psem, pcnt in b["r"].items():
                    need(psem, pcnt)
        if dma:
            pre = self.counts[sem]
            if pre > 0 and self.water[eng].get(sem, 0) < pre:
                needs[sem] = max(needs.get(sem, 0), pre)
        waits = []
        for psem in sorted(needs):
            pcnt = needs[psem]
            if self.water[eng].get(psem, 0) < pcnt:
                waits.append((psem, pcnt))
                self.water[eng][psem] = pcnt
        self.counts[sem] += amt
        cnt = self.counts[sem] + (0 if inc else 1)
        self.sched[eng].append((waits, fn, sem, amt))
        for k in writes:
            self.bufs[k] = {"w": (sem, cnt), "r": {}}
        for k in reads:
            b = self.bufs.setdefault(k, {"w": None, "r": {}})
            if b["r"].get(sem, 0) < cnt:
                b["r"][sem] = cnt
        return cnt

    def realize(self, block, sems, tail_waits):
        for e in ENGS:
            items = self.sched[e]
            tw = tail_waits.get(e, [])
            if not items and not tw:
                continue

            def mk(items, tw):
                def body(engobj):
                    for waits, fn, sem, amt in items:
                        for psem, pcnt in waits:
                            if psem.endswith("d"):
                                pcnt = min(pcnt, self.counts[psem])
                            engobj.wait_ge(sems[psem], pcnt)
                        inst = fn(engobj)
                        if amt:
                            inst.then_inc(sems[sem], amt)
                    for psem, pcnt in tw:
                        engobj.wait_ge(sems[psem], pcnt)
                return body
            getattr(block, e)(mk(items, tw))


def ts(i, n=128):
    return slice(i * n, (i + 1) * n)


def build(n_ttiles=NSEQ * NTT_PER_SEQ, ntt_per_seq=NTT_PER_SEQ, dbg=()):
    nc = bass.Bass("TRN2", target_bir_lowering=False)
    NTOK = n_ttiles * TT

    u_d = nc.dram_tensor("u", [NTOK, DM], F32, kind="ExternalInput")
    out_d = nc.dram_tensor("out", [NTOK, DM], F32, kind="ExternalOutput")
    winT_d = nc.dram_tensor("winT", [DM, DIN], F32, kind="ExternalInput")
    w0T_d = nc.dram_tensor("w0T", [DM, 8192], F32, kind="ExternalInput")
    w1T_d = nc.dram_tensor("w1T", [8192, DM], F32, kind="ExternalInput")
    woT_d = nc.dram_tensor("woT", [DI, DM], F32, kind="ExternalInput")
    cwp_d = nc.dram_tensor("cwp", [128, NFT * 4], F32, kind="ExternalInput")
    cbp_d = nc.dram_tensor("cbp", [128, NFT], F32, kind="ExternalInput")
    dtb_d = nc.dram_tensor("dtb", [NH, 1], F32, kind="ExternalInput")
    alog_d = nc.dram_tensor("alog", [NH, 1], F32, kind="ExternalInput")
    dp2_d = nc.dram_tensor("dp2", [128, 32], F32, kind="ExternalInput")
    nw_d = nc.dram_tensor("nw", [128, 32], F32, kind="ExternalInput")
    b0_d = nc.dram_tensor("b0c", [128, 64], F32, kind="ExternalInput")
    b1_d = nc.dram_tensor("b1c", [128, 16], F32, kind="ExternalInput")
    identb_d = nc.dram_tensor("identb", [128, 128], BF16, kind="ExternalInput")
    identf_d = nc.dram_tensor("identf", [128, 128], F32, kind="ExternalInput")
    maskrep_d = nc.dram_tensor("maskrep", [128, 512], F32, kind="ExternalInput")

    wq_0t = nc.dram_tensor("wq_0t", [128, 16 * 2 * 8 * 512], BF16)
    wq_int = nc.dram_tensor("wq_int", [128, 17 * 2 * 8 * 512], BF16)
    wq_1t = nc.dram_tensor("wq_1t", [128, 16 * 2 * 32 * 128], BF16)
    wq_ot = nc.dram_tensor("wq_ot", [128, 32 * 2 * 1024], BF16)
    dab = nc.dram_tensor("dab", [n_ttiles * NCH, NH, Q], F32)
    if "y1" in dbg:
        y1p_d = nc.dram_tensor("dbg_y1", [n_ttiles * NCH, 128, 4096], BF16,
                               kind="ExternalOutput")
    if "EMG" in dbg:
        emp_d = nc.dram_tensor("dbg_EM", [n_ttiles * NCH, 4, 128, 2048], BF16,
                               kind="ExternalOutput")
        gvp_d = nc.dram_tensor("dbg_Gv", [n_ttiles * NCH, 4, 128, 2048], BF16,
                               kind="ExternalOutput")
    gtm_q = nc.dram_tensor("gtm_q", [n_ttiles, TT, DM], BF16)

    p = Prog()
    es = ExitStack()
    sb = lambda name, shape, dt: es.enter_context(
        nc.sbuf_tensor("sb_" + name, shape, dt))

    identb = sb("identb", [128, 128], BF16)
    identf = sb("identf", [128, 128], F32)
    maskrep = sb("maskrep", [128, 512], F32)
    cwp = sb("cwp", [128, NFT * 4], F32)
    cbp = sb("cbp", [128, NFT], F32)
    dp2 = sb("dp2", [128, 32], F32)
    nwc = sb("nwc", [128, 32], F32)
    b0c = sb("b0c", [128, 64], F32)
    b1c = sb("b1c", [128, 16], F32)
    dtb = sb("dtb", [NH, 1], F32)
    alog = sb("alog", [NH, 1], F32)
    acol = sb("acol", [NH, 1], F32)
    ones1f = sb("ones1f", [1, 128], F32)
    onescb = sb("onescb", [128, 1], BF16)
    prevT = sb("prevT", [128, DI], F32)
    wbuf = sb("wbuf", [128, 16384], BF16)
    dtraw = sb("dtraw", [NH, TT], F32)
    dtv4 = sb("dtv4", [NH, TT], F32)
    dacs4 = sb("dacs4", [NH, TT], F32)
    lndt4 = sb("lndt4", [NH, TT], F32)
    wfm4 = sb("wfm4", [NH, TT], F32)
    wfb4 = sb("wfb4", [NH, TT], BF16)
    zq = sb("zq", [NH, Q], F32)
    biasEtm4 = sb("biasEtm4", [128, NCH * NH], F32)
    wtm4 = sb("wtm4", [128, NCH * NH], BF16)
    decN4 = sb("decN4", [128, NCH * NH], F32)
    drow = sb("drow", [1, NH], F32)
    meanc = sb("meanc", [128, 1], F32)
    nmean = sb("nmean", [128, 1], F32)
    sqs = sb("sqs", [128, 1], F32)
    varc = sb("varc", [128, 1], F32)
    rstdc = sb("rstdc", [128, 1], F32)
    rstd_t = sb("rstd_t", [1, 128], F32)
    rstd_c = sb("rstd_c", [128, 2], F32)
    tail = sb("tail", [128, NFT * 3], BF16)

    R1 = sb("R1", [128, 33792], BF16)
    R2 = sb("R2", [128, 32768], BF16)

    def carve(arena, aname, off_b, nelem, dt):
        if dt == BF16:
            ap = arena[:][:, off_b // 2: off_b // 2 + nelem]
        else:
            ap = arena[:].bitcast(F32)[:, off_b // 4: off_b // 4 + nelem]
        nbytes = nelem * (2 if dt == BF16 else 4)
        gran = 4096
        keys = [f"{aname}_{i}" for i in range(off_b // gran, (off_b + nbytes - 1) // gran + 1)]
        return ap, keys

    Dfm, kDfm = carve(R1, "R1", 0, 32768, BF16)
    zsfm, kzs = carve(R1, "R1", 0, 16384, BF16)
    accv, kacc = carve(R1, "R1", 32768, 17408, BF16)
    xnv, kxn = carve(R2, "R2", 0, 8192, BF16)
    ext, kext = carve(R2, "R2", 16384, NFT * (TT + 3), BF16)
    ctmp, kctmp = carve(R2, "R2", 53248, 4608, BF16)
    u_sub_t = sb("u_sub", [128, 2048], F32)
    u_sub, ku_s = u_sub_t[:], ["usub0", "usub1"]
    xn_tm, kxn_tm = carve(R2, "R2", 53248, 2048, BF16)
    gsil, kgsil = carve(R2, "R2", 16384, 8192, BF16)
    x_tm, kx_tm = carve(R2, "R2", 0, 4096, BF16)
    wxv, kwx = carve(R2, "R2", 8192, 4096, BF16)
    ygsq, kygsq = carve(R2, "R2", 8192, 4096, BF16)
    ygv, kyg = [], []
    for i in range(2):
        a, k = carve(R2, "R2", 16384 + i * 8192, 4096, BF16)
        ygv.append(a); kyg.append(k)
    EM, kEM = carve(R2, "R2", 32768, 2048, BF16)
    Gv, kGv = carve(R2, "R2", 36864, 2048, BF16)
    CBb, kCB = carve(R2, "R2", 40960, 128, BF16)
    Btm, kBtm = carve(R2, "R2", 41984, 128, BF16)
    outst = []; kout = []
    for i in range(2):
        a, k = carve(R2, "R2", 43008 + i * 8192, 2048, F32)
        outst.append(a); kout.append(k)
    prevb, kprevb = carve(R2, "R2", 59392, 1024, BF16)
    gtmr, kgtmr = carve(R2, "R2", 61440, 2048, BF16)
    u_rel, ku_rel = carve(R2, "R2", 16384, 2048, F32)

    pb = [es.enter_context(nc.psum_tensor(f"pb{i}", [128, 512], F32)) for i in range(6)]
    pt = [es.enter_context(nc.psum_tensor(f"pt{i}", [128, 1024], BF16)) for i in range(2)]

    def pk(bank, c0, c1):
        return [f"P{bank}_{i}" for i in range(c0 // 128, (c1 - 1) // 128 + 1)] + [
            f"PB{bank}"]

    sems = {}
    for e in ENGS:
        sems[e] = es.enter_context(nc.semaphore("s_" + e))
        sems[e + "d"] = es.enter_context(nc.semaphore("s_" + e + "d"))

    dbg_items = []

    def dump(name, ap, shape, dt, rkeys=()):
        if name in dbg:
            d = nc.dram_tensor("dbg_" + name, shape, dt, kind="ExternalOutput")
            dbg_items.append((name, d, ap, list(rkeys)))

    for key, dst, src in [("identb", identb, identb_d), ("identf", identf, identf_d),
                          ("maskrep", maskrep, maskrep_d),
                          ("cwp", cwp, cwp_d), ("cbp", cbp, cbp_d),
                          ("dp2", dp2, dp2_d), ("nwc", nwc, nw_d),
                          ("b0c", b0c, b0_d), ("b1c", b1c, b1_d),
                          ("dtb", dtb, dtb_d), ("alog", alog, alog_d)]:
        p.emit("gpsimd", lambda g, a=dst, b=src: g.dma_start(a[:], b[:]),
               writes=[key], dma=True)
    p.emit("vector", lambda e: e.memset(ones1f[:], 1.0), writes=["ones1f"])
    p.emit("vector", lambda e: e.memset(onescb[:], 1.0), writes=["onescb"])
    p.emit("vector", lambda e: e.memset(zq[:], 0.0), writes=["zq"])
    p.emit("vector", lambda e: e.memset(prevT[:], 0.0), writes=["prevT"])
    p.emit("vector", lambda e: e.memset(tail[:], 0.0), writes=["tail"])
    p.emit("scalar", lambda e: e.activation(acol[:], alog[:], AF.Exp),
           reads=["alog"], writes=["acol_e"])
    p.emit("vector", lambda e: e.tensor_scalar_mul(acol[:], acol[:], -1.0),
           reads=["acol_e"], writes=["acol"])

    w0v = wq_0t[:].rearrange("p (g h k w) -> p g h k w", g=16, h=2, k=8)
    winv = wq_int[:].rearrange("p (g h k w) -> p g h k w", g=17, h=2, k=8)
    w1v = wq_1t[:].rearrange("p (n h k w) -> p n h k w", n=16, h=2, k=32)
    wov = wq_ot[:].rearrange("p (n k w) -> p n k w", n=2, k=32)

    cast_srcs = [carve(R2, "R2", 32768, 2048, F32),
                 carve(R2, "R2", 40960, 2048, F32)]
    cast_dsts = [carve(R2, "R2", 49152, 2048, BF16),
                 carve(R2, "R2", 53248, 2048, BF16)]
    cast_ctr = [0]

    def cast_piece(qkey, src_ap, dst_ap, dshape=None, fold_col=None):
        i = cast_ctr[0]; cast_ctr[0] += 1
        pr = i % 2
        n = src_ap.shape[1]
        (sap, kcs2), (dap, kcd2) = cast_srcs[pr], cast_dsts[pr]
        s = sap[:, 0:n]
        d = dap[:, 0:n]
        dd = d if dshape is None else d.rearrange(*dshape[0], **dshape[1])
        q = "sync" if pr == 0 else "gpsimd"
        p.emit(q, lambda g, s=s, src_ap=src_ap: g.dma_start(s, src_ap),
               reads=[f"cd{pr}"] + kcd2 + kcs2, writes=[f"cs{pr}"] + kcs2,
               dma=True)
        if fold_col is not None:
            p.emit("vector", lambda e, s=s, d=d, fc=fold_col:
                   e.tensor_scalar_mul(d, s, fc),
                   reads=[f"cs{pr}", "nwc"] + kcs2 + kcd2,
                   writes=[f"cd{pr}"] + kcd2)
        else:
            p.emit("vector", lambda e, s=s, d=d: e.tensor_copy(d, s),
                   reads=[f"cs{pr}"] + kcs2 + kcd2, writes=[f"cd{pr}"] + kcd2)
        p.emit("scalar", lambda g, dst_ap=dst_ap, dd=dd: g.dma_start(
            dst_ap, dd), reads=[f"cd{pr}"] + kcd2, writes=[qkey], dma=True)

    pp_tasks = []

    def pump(n):
        for _ in range(min(n, len(pp_tasks))):
            pp_tasks.pop(0)()

    def _emit_prepass():
        for kg in range(64):
            kh, kk = kg // 32, kg % 32
            pp_tasks.append(lambda kg=kg, kh=kh, kk=kk: cast_piece(
                "wq1_ready", w1T_d[ts(kg), :],
                w1v[:, :, kh, kk, :], (("p (n w) -> p n w",), {"n": 16})))
        for r in range(32):
            pp_tasks.append(lambda r=r: cast_piece(
                "wqo_ready", woT_d[ts(r), :],
                wov[:, :, r, :], (("p (n w) -> p n w",), {"n": 2}),
                fold_col=nwc[:, r:r + 1]))


    def wbk(lo, hi):
        return [f"WB_{i}" for i in range(lo // 1024, (hi - 1) // 1024 + 1)]

    def load_wtile(view_slice, half, rkey, par):
        lo = par * 8192 + half * 4096
        dst = wbuf[:, lo: lo + 4096]
        q = "sync" if half == 0 else "gpsimd"
        p.emit(q, lambda g, dst=dst, s=view_slice: g.dma_start(dst, s),
               reads=[rkey], writes=wbk(lo, lo + 4096), dma=True)

    def wsl(half, kh, blockw, off, length, par):
        base = par * 8192 + half * 4096 + kh * blockw + off
        return wbuf[:, base: base + length], wbk(base, base + length)

    def direct_cast(wT_d, wview, grp, gpar, skey, eng, ncols=512, npad=0):
        col0 = grp * 512 if npad == 0 else 8192
        for r in range(16):
            half, kt = r // 8, r % 8
            pr = r % 2
            s = u_sub[:, pr * 512: pr * 512 + ncols]
            sk = [f"dcS{pr}"]
            base = gpar * 8192 + half * 4096 + kt * 512
            dst = wbuf[:, base: base + ncols]
            dk = wbk(base, base + 512)
            q = "sync" if pr == 0 else "gpsimd"
            p.emit(q, lambda g, s=s, src=wT_d[ts(r), col0: col0 + ncols]:
                   g.dma_start(s, src), reads=sk, writes=sk, dma=True)
            if eng == "vector":
                p.emit(eng, lambda e, s=s, d=dst: e.tensor_copy(d, s),
                       reads=sk + dk, writes=dk + ["dccp"])
            else:
                p.emit(eng, lambda e, s=s, d=dst: e.copy(d, s),
                       reads=sk + dk, writes=dk + ["dccp"])
        for half in range(2):
            lo = gpar * 8192 + half * 4096
            if npad:
                w8 = wbuf[:, lo: lo + 4096].rearrange("p (k w) -> p k w", k=8)
                p.emit("vector", lambda e, w8=w8, npad=npad: e.memset(
                    w8[:, :, 512 - npad: 512], 0.0),
                    reads=wbk(lo, lo + 4096), writes=wbk(lo, lo + 4096))
            p.emit("scalar", lambda g, d=wview[:, grp, half],
                   s=wbuf[:, lo: lo + 4096]: g.dma_start(d, s),
                   reads=wbk(lo, lo + 4096), writes=[skey], dma=True)

    tposed = [0]

    def transpose_b(src_ap, dst_ap, rkeys, wkeys):
        i = tposed[0]; tposed[0] += 1
        bk, sl = (i // 8) % 2, i % 8
        key = f"T{bk}_{sl}"
        outp = pt[bk][0: src_ap.shape[1], sl * 128: sl * 128 + src_ap.shape[0]]
        idn = identb[0:src_ap.shape[0], 0:src_ap.shape[0]]
        p.emit("tensor", lambda e, s=src_ap, d=outp, idn=idn: e.transpose(d, s, idn),
               reads=list(rkeys) + ["identb"], writes=[key, f"TB{bk}"])
        if sl % 2 == 0:
            p.emit("scalar", lambda e, s=outp, d=dst_ap: e.copy(d, s),
                   reads=[key, f"TB{bk}"], writes=list(wkeys))
        else:
            p.emit("vector", lambda e, s=outp, d=dst_ap: e.tensor_copy(d, s),
                   reads=[key, f"TB{bk}"], writes=list(wkeys))

    for tt in range(n_ttiles):
        seq_start = (tt % ntt_per_seq) == 0
        t0 = tt * TT

        for s in range(NSUB):
            p.emit("gpsimd", lambda g, s=s, t0=t0: g.dma_start(
                u_sub, u_d[t0 + s * 128: t0 + (s + 1) * 128, :]),
                reads=ku_s, writes=ku_s + ["u_sub"], dma=True)
            p.emit("vector", lambda e: e.reduce_sum(meanc[:], u_sub, axis=AX.X),
                   reads=ku_s + ["u_sub"], writes=["meanc_s"])
            p.emit("scalar", lambda e: e.activation(
                outst[0], u_sub, AF.Square, accum_out=sqs[:]),
                reads=ku_s + ["u_sub"] + kout[0], writes=kout[0] + ["sqs"])
            p.emit("vector", lambda e: e.tensor_scalar_mul(meanc[:], meanc[:], 1.0 / DM),
                   reads=["meanc_s"], writes=["meanc"])
            p.emit("vector", lambda e: e.tensor_scalar_mul(nmean[:], meanc[:], -1.0),
                   reads=["meanc"], writes=["nmean"])
            p.emit("vector", lambda e: e.tensor_scalar(
                varc[:], sqs[:], 1.0 / DM, EPS, ALU.mult, ALU.add),
                reads=["sqs"], writes=["varc_s"])
            p.emit("vector", lambda e: e.scalar_tensor_tensor(
                varc[:], meanc[:], nmean[:, 0:1], varc[:], ALU.mult, ALU.add),
                reads=["meanc", "nmean", "varc_s"], writes=["varc"])
            p.emit("scalar", lambda e: e.activation(varc[:], varc[:], AF.Sqrt),
                   reads=["varc"], writes=["varc_q"])
            p.emit("vector", lambda e: e.reciprocal(rstdc[:], varc[:]),
                   reads=["varc_q"], writes=["rstdc"])
            p.emit("vector", lambda e: e.tensor_scalar(
                xn_tm, u_sub, meanc[:], rstdc[:], ALU.subtract, ALU.mult),
                reads=ku_s + ["u_sub", "meanc", "rstdc"] + kxn_tm,
                writes=kxn_tm + ["xn_tm"])
            for kt in range(16):
                dst = xnv[:, kt * TT + s * 128: kt * TT + (s + 1) * 128]
                transpose_b(xn_tm[:, ts(kt)], dst, kxn_tm + ["xn_tm"],
                            kxn + ["xnv"])
        dump("xnfm", xnv, [128, 16 * TT], BF16, ["xnv"] + kxn)

        if tt == 0:
            _emit_prepass()

        if tt == 0:
            p.emit("vector", lambda e: e.engine_nop(),
                   writes=["u_sub", "usub0", "usub1", "dcS0", "dcS1"])
        for grp in range(16):
            gpar = grp % 2
            if tt == 0:
                direct_cast(w0T_d, w0v, grp, gpar, "wq0_ready", "vector")
            else:
                for half in range(2):
                    load_wtile(w0v[:, grp, half], half, "wq0_ready", gpar)
            for ti in range(4):
                dt_i = grp * 4 + ti
                bank = dt_i % 3
                bkeys = pk(bank, 0, 512)
                for k in range(16):
                    half, kh = divmod(k, 8)
                    l, lk = wsl(half, kh, 512, ti * 128, 128, gpar)
                    p.emit("tensor", lambda e, l=l, k=k,
                           b=bank: e.matmul(pb[b][:], l, xnv[:, ts(k, TT)],
                                            start=(k == 0), stop=(k == 15)),
                           reads=lk + ["xnv"] + kxn, writes=bkeys,
                           inc=(k == 15))
                ddst = Dfm[:, dt_i * TT: (dt_i + 1) * TT]
                p.emit("scalar", lambda e, d=ddst, b=bank, dt_i=dt_i: e.activation(
                    d, pb[b][:], AF.Gelu, bias=b0c[:, dt_i:dt_i + 1]),
                    reads=bkeys + ["b0c"], writes=kDfm + ["Dfm"])
            if tt == 0:
                pump(4)
        dump("Dfm", Dfm, [128, 64 * TT], BF16, ["Dfm"] + kDfm)

        for nt in range(16):
            bank = nt % 3
            bkeys = pk(bank, 0, 512)
            npar = nt % 2
            for kh in range(2):
                load_wtile(w1v[:, nt, kh], kh, "wq1_ready", npar)
                for k in range(32):
                    kg = kh * 32 + k
                    l, lk = wsl(kh, k, 128, 0, 128, npar)
                    p.emit("tensor", lambda e, l=l, kg=kg,
                           b=bank: e.matmul(pb[b][:], l, Dfm[:, ts(kg, TT)],
                                            start=(kg == 0), stop=(kg == 63)),
                           reads=lk + ["Dfm"] + kDfm, writes=bkeys,
                           inc=(kg == 63))
            gdst = gsil[:, nt * TT: (nt + 1) * TT]
            p.emit("scalar", lambda e, d=gdst, b=bank, nt=nt: e.activation(
                d, pb[b][:], AF.Silu, bias=b1c[:, nt:nt + 1]),
                reads=bkeys + ["b1c"], writes=kgsil + ["gsil"])
            if tt == 0:
                pump(2)
        for s in range(NSUB):
            for nt in range(16):
                transpose_b(gsil[:, nt * TT + s * 128: nt * TT + (s + 1) * 128],
                            gtmr[:, ts(nt)], kgsil + ["gsil"], kgtmr + ["gtmr"])
            p.emit("scalar", lambda g, s=s, tt=tt: g.dma_start(
                gtm_q[tt, s * 128:(s + 1) * 128, :], gtmr),
                reads=kgtmr + ["gtmr"], writes=[f"gq{tt}_{s}"], dma=True)
        dump("gsil", gsil, [128, 16 * TT], BF16, ["gsil"] + kgsil)

        ext3_ = ext.rearrange("p (ft c) -> p ft c", ft=NFT)
        acc3_ = accv.rearrange("p (ft c) -> p ft c", ft=NFT)

        def extk(ft0, gsz):
            lo = 16384 + ft0 * (TT + 3) * 2
            hi = 16384 + (ft0 + gsz) * (TT + 3) * 2
            return [f"R2_{i}" for i in range(lo // 4096, (hi - 1) // 4096 + 1)]

        def acck(ft0, gsz):
            lo = 32768 + ft0 * TT * 2
            hi = 32768 + (ft0 + gsz) * TT * 2
            return [f"R1_{i}" for i in range(lo // 4096, (hi - 1) // 4096 + 1)]

        def conv_group(ft0, gsz, seq_start=seq_start):
            ek = extk(ft0, gsz)
            ak = acck(ft0, gsz)
            if seq_start:
                p.emit("vector", lambda e, ft0=ft0, gsz=gsz: e.memset(
                    ext3_[:, ft0:ft0 + gsz, 0:3], 0.0),
                    reads=ek, writes=ek + ["ext_h"])
            else:
                p.emit("vector", lambda e, ft0=ft0, gsz=gsz: e.tensor_copy(
                    ext3_[:, ft0:ft0 + gsz, 0:3],
                    tail[:].rearrange("p (f c) -> p f c", f=NFT)[:, ft0:ft0 + gsz, :]),
                    reads=["tail"] + ek, writes=ek + ["ext_h"])
            for w in range(4):
                s = ext3_[:, ft0: ft0 + gsz, w: w + TT]
                cw = cwp[:, ft0 * 4 + w: (ft0 + gsz - 1) * 4 + w + 1: 4]
                cwb = cw.unsqueeze(2).broadcast_to([128, gsz, TT])
                ac = acc3_[:, ft0: ft0 + gsz, :]
                if w == 0:
                    p.emit("vector", lambda e, s=s, c=cwb, a=ac: e.tensor_tensor(
                        a, s, c, ALU.mult),
                        reads=ek + ["ext_h", "cwp"] + ak,
                        writes=ak + ["acc_p"])
                else:
                    t3 = ctmp[:, 0: gsz * TT].rearrange("p (f c) -> p f c", f=gsz)
                    p.emit("vector", lambda e, s=s, c=cwb, t=t3: e.tensor_tensor(
                        t, s, c, ALU.mult),
                        reads=ek + ["ext_h", "cwp", "gefence"] + kctmp,
                        writes=kctmp + ["ctmp"])
                    p.emit("vector", lambda e, t=t3, a=ac: e.tensor_tensor(
                        a, a, t, ALU.add),
                        reads=kctmp + ["ctmp", "acc_p"] + ak,
                        writes=ak + ["acc_p"])
            for ft in range(ft0, ft0 + gsz):
                a = accv[:, ft * TT: (ft + 1) * TT]
                p.emit("scalar", lambda e, a=a, ft=ft: e.activation(
                    a, a, AF.Silu, bias=cbp[:, ft:ft + 1]),
                    reads=["acc_p", "cbp"] + ak, writes=ak + ["accv"])

        def _emit_dt_hoist():
            p.emit("vector", lambda e: e.tensor_scalar_add(
                dtraw[:], dtraw[:], dtb[:, 0:1]),
                reads=["dtraw", "dtb"], writes=["dtraw"])
            p.emit("scalar", lambda e: e.activation(dtv4[:], dtraw[:], AF.Abs),
                   reads=["dtraw"], writes=["dt_abs"])
            p.emit("scalar", lambda e: e.activation(dtv4[:], dtv4[:], AF.Exp, scale=-1.0),
                   reads=["dt_abs"], writes=["dt_exp"])
            p.emit("vector", lambda e: e.tensor_scalar_add(dtv4[:], dtv4[:], 1.0),
                   reads=["dt_exp"], writes=["dt_e1"])
            p.emit("scalar", lambda e: e.activation(dtv4[:], dtv4[:], AF.Ln),
                   reads=["dt_e1"], writes=["dt_l1p"])
            p.emit("vector", lambda e: e.tensor_scalar_max(dacs4[:], dtraw[:], 0.0),
                   reads=["dtraw"], writes=["dt_mx"])
            p.emit("vector", lambda e: e.tensor_add(dtv4[:], dtv4[:], dacs4[:]),
                   reads=["dt_mx", "dt_l1p"], writes=["dtv4"])
            p.emit("vector", lambda e: e.tensor_scalar_mul(
                dtraw[:], dtv4[:], acol[:, 0:1]),
                reads=["dtv4", "acol", "dtraw"], writes=["dtA4"])
            for c in range(NCH):
                cg = tt * NCH + c
                tc = c * Q
                p.emit("vector", lambda e, tc=tc: e.tensor_tensor_scan(
                    dacs4[:, tc:tc + Q], dtraw[:, tc:tc + Q], zq[:], 0.0,
                    ALU.add, ALU.add),
                    reads=["dtA4", "zq", "dt_mx"], writes=["dacs4"])
                p.emit("gpsimd", lambda g, cg=cg, tc=tc: g.dma_start(
                    dab[cg], dacs4[:, tc:tc + Q]),
                    reads=["dacs4"], writes=[f"dab{cg}"], dma=True)
            p.emit("scalar", lambda e: e.activation(lndt4[:], dtv4[:], AF.Ln),
                   reads=["dtv4"], writes=["lndt4"])
            p.emit("vector", lambda e: e.tensor_sub(lndt4[:], lndt4[:], dacs4[:]),
                   reads=["lndt4", "dacs4"], writes=["biasE4"])
            for c in range(NCH):
                tc = c * Q
                p.emit("scalar", lambda e, tc=tc: e.activation(
                    wfm4[:, tc:tc + Q], dacs4[:, tc:tc + Q], AF.Exp,
                    bias=dacs4[:, tc + Q - 1:tc + Q], scale=-1.0),
                    reads=["dacs4"], writes=["wfm4e"])
            p.emit("vector", lambda e: e.tensor_mul(wfm4[:], wfm4[:], dtv4[:]),
                   reads=["wfm4e", "dtv4"], writes=["wfm4"])
            p.emit("vector", lambda e: e.tensor_copy(wfb4[:], wfm4[:]),
                   reads=["wfm4"], writes=["wfb4"])
            for c in range(NCH):
                tc = c * Q
                p.emit("tensor", lambda e, tc=tc: e.transpose(
                    pb[5][:, 0:64], lndt4[:, tc:tc + Q], identf[0:64, 0:64]),
                    reads=["biasE4", "identf"], writes=pk(5, 0, 128))
                p.emit("vector", lambda e, c=c: e.tensor_copy(
                    biasEtm4[:, c * NH:(c + 1) * NH], pb[5][:, 0:64]),
                    reads=pk(5, 0, 128), writes=["biasEtm4"])
                transpose_b(wfb4[:, tc:tc + Q], wtm4[:, c * NH:(c + 1) * NH],
                            ["wfb4"], ["wtm4"])
                p.emit("tensor", lambda e, tc=tc: e.matmul(
                    pb[5][0:1, 128:192], dacs4[:, tc + Q - 1:tc + Q],
                    identf[0:64, 0:64], start=True, stop=True),
                    reads=["dacs4", "identf"], writes=pk(5, 128, 192))
                p.emit("vector", lambda e: e.tensor_copy(drow[:], pb[5][0:1, 128:192]),
                       reads=pk(5, 128, 192), writes=["drow"])
                p.emit("tensor", lambda e: e.matmul(
                    pb[5][:, 192:256], ones1f[:], drow[:], start=True, stop=True),
                    reads=["drow", "ones1f"], writes=pk(5, 192, 256))
                p.emit("scalar", lambda e, c=c: e.activation(
                    decN4[:, c * NH:(c + 1) * NH], pb[5][:, 192:256], AF.Exp),
                    reads=pk(5, 192, 256), writes=["decN4"])


        conv_sched = {10: (0, 9), 12: (9, 9), 14: (18, 8), 15: (26, 8)}
        p.emit("vector", lambda e: e.engine_nop(),
               reads=["gv0", "gv1", "em0", "em1", "pvb0", "pvb1",
                      "bcast0", "bcast1", "EM", "Gv"],
               writes=["gefence"])
        if tt == 0:
            pump(len(pp_tasks))
        for grp in [16] + list(range(16)):
            ntile = 4 if grp < 16 else 3
            gpar = grp % 2
            if tt == 0:
                direct_cast(winT_d, winv, grp, gpar, "wqi_ready", "scalar",
                            ncols=512 if grp < 16 else 320,
                            npad=0 if grp < 16 else 192)
            else:
                for half in range(2):
                    load_wtile(winv[:, grp, half], half, "wqi_ready", gpar)
            for ti in range(ntile):
                dt_i = grp * 4 + ti
                if dt_i == 66:
                    bkeys = pk(4, 0, 512)
                    for k in range(16):
                        half, kh = divmod(k, 8)
                        l, lk = wsl(half, kh, 512, ti * 128, 64, gpar)
                        p.emit("tensor", lambda e, l=l, k=k: e.matmul(
                            pb[4][0:64, :], l, xnv[:, ts(k, TT)],
                            start=(k == 0), stop=(k == 15)),
                            reads=lk + ["xnv"] + kxn, writes=bkeys,
                            inc=(k == 15))
                    p.emit("vector", lambda e: e.tensor_copy(
                        dtraw[:], pb[4][0:64, :]),
                        reads=bkeys, writes=["dtraw"])
                    _emit_dt_hoist()
                    continue
                bank = dt_i % 3
                bkeys = pk(bank, 0, 512)
                for k in range(16):
                    half, kh = divmod(k, 8)
                    l, lk = wsl(half, kh, 512, ti * 128, 128, gpar)
                    p.emit("tensor", lambda e, l=l, k=k,
                           b=bank: e.matmul(pb[b][:], l, xnv[:, ts(k, TT)],
                                            start=(k == 0), stop=(k == 15)),
                           reads=lk + ["xnv"] + kxn, writes=bkeys,
                           inc=(k == 15))
                if dt_i < 32:
                    zdst = zsfm[:, dt_i * TT: (dt_i + 1) * TT]
                    p.emit("scalar", lambda e, d=zdst, b=bank: e.activation(
                        d, pb[b][:], AF.Silu), reads=bkeys,
                        writes=kzs + ["zsfm"])
                else:
                    ft = dt_i - 32
                    edst = ext[:, ft * (TT + 3) + 3: ft * (TT + 3) + 3 + TT]
                    p.emit("vector", lambda e, d=edst, b=bank: e.tensor_copy(
                        d, pb[b][:]), reads=bkeys + ["gefence"],
                        writes=extk(ft, 1) + ["ext"])
            if grp in conv_sched:
                conv_group(*conv_sched[grp])
        if tt == 0:
            p.emit("vector", lambda e: e.engine_nop(),
                   reads=["dccp"],
                   writes=["u_sub", "usub0", "usub1"])
        if "xne" in dbg and tt == 0:
            xned = nc.dram_tensor("dbg_xne", [128, 16 * TT], BF16,
                                  kind="ExternalOutput")
            p.emit("gpsimd", lambda g: g.dma_start(xned[:], xnv),
                   reads=["xnv"] + kxn, writes=["xnedump"], dma=True)
        dump("zsfm", zsfm, [128, 32 * TT], BF16, ["zsfm"] + kzs)
        dump("dtraw", dtraw[:], [NH, TT], F32, ["dtraw"])

        ext3 = ext.rearrange("p (ft c) -> p ft c", ft=NFT)
        p.emit("vector", lambda e, ext3=ext3: e.tensor_copy(
            tail[:].rearrange("p (ft c) -> p ft c", ft=NFT), ext3[:, :, TT: TT + 3]),
            reads=kext + ["ext", "ext_h"], writes=["tail"])
        dump("accv", accv, [128, NFT * TT], BF16, ["accv"] + kacc)

        p.emit("vector", lambda e: e.engine_nop(),
               reads=kGv + kEM + kprevb + ["ext", "accv"],
               writes=["ssdfence"])
        for c in range(NCH):
            cg = tt * NCH + c
            tc = c * Q
            par = c % 2
            def wo_load(dq):
                nh2q, k2 = divmod(dq, 16)
                slot = dq % 8
                wosl_ = wbuf[:, slot * 2048: slot * 2048 + 2048]
                wok_ = wbk(slot * 2048, slot * 2048 + 2048)
                qs = ("sync", "scalar", "gpsimd")[dq % 3]
                p.emit(qs, lambda g, w=wosl_, k2=k2, nh2q=nh2q: g.dma_start(
                           w, wov[:, nh2q, 2 * k2: 2 * k2 + 2]),
                       reads=["wqo_ready"], writes=wok_, dma=True)

            if par == 0:
                for dq in range(8):
                    wo_load(dq)
            Bfm = accv[:, 32 * TT + tc: 32 * TT + tc + Q]
            Cfm = accv[:, 33 * TT + tc: 33 * TT + tc + Q]
            transpose_b(Bfm, Btm, kacc + ["accv"], kBtm + ["Btm"])
            for ftx in range(32):
                transpose_b(accv[:, ftx * TT + tc: ftx * TT + tc + Q],
                            x_tm[:, ts(ftx)], kacc + ["accv"], kx_tm + ["x_tm"])
            wtm_b = wtm4[:, c * NH:(c + 1) * NH].unsqueeze(2).broadcast_to(
                [128, NH, DH])
            p.emit("vector", lambda e, wb=wtm_b: e.tensor_tensor(
                wxv.rearrange("p (h d) -> p h d", h=NH),
                x_tm.rearrange("p (h d) -> p h d", h=NH), wb, ALU.mult),
                reads=kx_tm + ["x_tm", "wtm4"] + kwx, writes=kwx + ["wxv"])
            p.emit("tensor", lambda e, B=Bfm, C=Cfm: e.matmul(
                pb[4][:, 0:128], B, C, start=True, stop=True),
                reads=kacc + ["accv"], writes=pk(4, 0, 128))
            p.emit("vector", lambda e: e.tensor_copy(CBb, pb[4][:, 0:128]),
                   reads=pk(4, 0, 128), writes=kCB + ["CBb"])

            def pre_dma(s8):
                sp = s8 % 2
                uhalf = u_sub[:, sp * 1024:(sp + 1) * 1024]
                kb = f"usub{sp}"
                bsrc = dab[cg].rearrange("h q -> (h q)").unsqueeze(0)[
                    0:1, s8 * 1024: (s8 + 1) * 1024].partition_broadcast(128)
                p.emit("gpsimd", lambda g, bsrc=bsrc, uhalf=uhalf: g.dma_start(
                    uhalf, bsrc),
                    reads=[f"dab{cg}", kb], writes=[kb, f"bcast{sp}"], dma=True)

            def pre_exp(s8):
                sp = s8 % 2
                uhalf = u_sub[:, sp * 1024:(sp + 1) * 1024]
                gvh = Gv[:, sp * 1024:(sp + 1) * 1024]
                kb = f"usub{sp}"; kg_ = f"gv{sp}"
                p.emit("scalar", lambda e, gvh=gvh, uhalf=uhalf: e.activation(
                    gvh, uhalf, AF.Exp),
                    reads=[kb, f"bcast{sp}", "ssdfence"],
                    writes=[kg_, kg_ + "g"])

            def ssd_front(s8):
                sp = s8 % 2
                h0 = s8 * 8
                uhalf = u_sub[:, sp * 1024:(sp + 1) * 1024]
                gvh = Gv[:, sp * 1024:(sp + 1) * 1024]
                emh = EM[:, sp * 1024:(sp + 1) * 1024]
                pvh = prevb[:, sp * 512:(sp + 1) * 512]
                kb = f"usub{sp}"; kg_ = f"gv{sp}"; ke_ = f"em{sp}"
                kp_ = f"pvb{sp}"
                u2v = uhalf.rearrange("p (o w) -> p o w", o=2)
                mrep = maskrep[:].unsqueeze(1).broadcast_to([128, 2, 512])
                p.emit("vector", lambda e, u2v=u2v, mrep=mrep: e.tensor_tensor(
                    u2v, u2v, mrep, ALU.add),
                    reads=["maskrep", kg_, f"bcast{sp}", kb],
                    writes=[kb, f"bcastm{sp}"])
                u8v = uhalf.rearrange("p (h w) -> p h w", h=8)
                bsl = biasEtm4[:, c * NH + h0: c * NH + h0 + 8]
                bslb = bsl.unsqueeze(2).broadcast_to([128, 8, Q])
                p.emit("vector", lambda e, u8v=u8v, bslb=bslb: e.tensor_tensor(
                    u8v, u8v, bslb, ALU.add),
                    reads=["biasEtm4", f"bcastm{sp}", kb],
                    writes=[kb, f"bcastb{sp}"])
                p.emit("scalar", lambda e, emh=emh, uhalf=uhalf: e.activation(
                    emh, uhalf, AF.Exp),
                    reads=[kb, f"bcastb{sp}", "ssdfence"],
                    writes=[ke_ + "e", ke_])
                cb = Cfm.unsqueeze(1).broadcast_to([128, 8, Q])
                gv3 = gvh.rearrange("p (h q) -> p h q", h=8)
                p.emit("vector", lambda e, gv3=gv3, cb=cb: e.tensor_tensor(
                    gv3, gv3, cb, ALU.mult),
                    reads=[kg_, "accv"] + kacc, writes=[kg_ + "g"])
                cbb = CBb.unsqueeze(1).broadcast_to([128, 8, Q])
                em3 = emh.rearrange("p (h q) -> p h q", h=8)
                p.emit("vector", lambda e, em3=em3, cbb=cbb: e.tensor_tensor(
                    em3, em3, cbb, ALU.mult),
                    reads=[ke_ + "e", "CBb"] + kCB, writes=[ke_])
                p.emit("scalar", lambda e, pvh=pvh, h0=h0: e.copy(
                    pvh, prevT[:, h0 * DH: (h0 + 8) * DH]),
                    reads=["prevT", kp_], writes=[kp_])

            def ssd_back(s8):
                sp = s8 % 2
                h0 = s8 * 8
                gvh = Gv[:, sp * 1024:(sp + 1) * 1024]
                emh = EM[:, sp * 1024:(sp + 1) * 1024]
                pvh = prevb[:, sp * 512:(sp + 1) * 512]
                kg_ = f"gv{sp}"; ke_ = f"em{sp}"; kp_ = f"pvb{sp}"
                ybank = 4 + sp
                for hh in range(8):
                    h = h0 + hh
                    pair, pr = divmod(hh, 2)
                    off = pair * 128
                    yp = pb[ybank][pr * 64:(pr + 1) * 64, off:off + 128]
                    ykeys = pk(ybank, off, off + 128)
                    p.emit("tensor", lambda e, yp=yp, hh=hh, pvh=pvh, gvh=gvh:
                        e.matmul(yp, pvh[:, ts(hh, DH)], gvh[:, ts(hh)],
                                 start=True, stop=False),
                        reads=[kp_, kg_ + "g"], writes=ykeys, inc=False)
                    p.emit("tensor", lambda e, yp=yp, h=h, hh=hh, emh=emh:
                        e.matmul(yp, x_tm[:, ts(h, DH)], emh[:, ts(hh)],
                                 start=False, stop=True),
                        reads=kx_tm + ["x_tm", ke_], writes=ykeys)
                for pair in range(4):
                    gpair = s8 * 4 + pair
                    off = pair * 128
                    yp = pb[ybank][:, off:off + 128]
                    xsl = accv[:, gpair * TT + tc: gpair * TT + tc + Q]
                    p.emit("vector", lambda e, yp=yp, xsl=xsl, gpair=gpair,
                           par=par:
                        e.scalar_tensor_tensor(
                            ygv[par][:, ts(gpair)], xsl, dp2[:, gpair:gpair + 1],
                            yp, ALU.mult, ALU.add),
                        reads=pk(ybank, off, off + 128) + kacc +
                        ["accv", "dp2"] + kyg[par],
                        writes=kyg[par] + [f"yg{par}"])
                p.emit("tensor", lambda e, sp=sp, s8=s8: e.matmul(
                    pb[sp][:], Btm, wxv[:, s8 * 512: (s8 + 1) * 512],
                    start=True, stop=True),
                    reads=kBtm + ["Btm", "wxv"] + kwx, writes=pk(sp, 0, 512))
                dsl = decN4[:, c * NH + h0: c * NH + h0 + 8]
                db = dsl.unsqueeze(2).broadcast_to([128, 8, DH])
                pv = prevT[:, h0 * DH: (h0 + 8) * DH]
                pv3 = pv.rearrange("p (h d) -> p h d", h=8)
                p.emit("gpsimd", lambda e, pv3=pv3, db=db: e.tensor_tensor(
                    pv3, pv3, db, ALU.mult),
                    reads=["prevT", "decN4", kp_], writes=["prevT_s"])
                p.emit("vector", lambda e, pv=pv, sp=sp: e.tensor_tensor(
                    pv, pv, pb[sp][:, 0:512], ALU.add),
                    reads=["prevT_s"] + pk(sp, 0, 512), writes=["prevT"])


            pre_dma(0)
            pre_exp(0)
            for s8 in range(9):
                if s8 + 1 < 8:
                    pre_dma(s8 + 1)
                if s8 < 8:
                    ssd_front(s8)
                if s8 >= 1:
                    ssd_back(s8 - 1)
                if s8 + 1 < 8:
                    pre_exp(s8 + 1)

            if "y1" in dbg:
                p.emit("gpsimd", lambda g, cg=cg, par=par: g.dma_start(
                    y1p_d[cg], ygv[par]), reads=kyg[par] + [f"yg{par}"],
                    writes=["y1pdbg"], dma=True)
            zsl = zsfm.rearrange("p (f c) -> p f c", f=32)[:, :, tc:tc + Q]
            y3 = ygv[par].rearrange("p (f c) -> p f c", f=32)
            p.emit("vector", lambda e, zsl=zsl, y3=y3: e.tensor_tensor(
                y3, y3, zsl, ALU.mult),
                reads=[f"yg{par}", "zsfm"] + kzs + kyg[par],
                writes=kyg[par] + [f"yg{par}"])
            p.emit("scalar", lambda e, par=par: e.activation(
                ygsq, ygv[par], AF.Square),
                reads=[f"yg{par}"] + kyg[par] + kygsq, writes=kygsq + ["ygsq"])
            for ftx in range(32):
                p.emit("tensor", lambda e, ftx=ftx: e.matmul(
                    pb[4][0:1, 256:384], onescb[:], ygsq[:, ts(ftx)],
                    start=(ftx == 0), stop=(ftx == 31)),
                    reads=kygsq + ["ygsq", "onescb"], writes=pk(4, 256, 384),
                    inc=(ftx == 31))
            p.emit("vector", lambda e: e.tensor_scalar(
                rstd_t[:], pb[4][0:1, 256:384], 1.0 / DI, EPS, ALU.mult, ALU.add),
                reads=pk(4, 256, 384), writes=["rstd_ts"])
            p.emit("scalar", lambda e: e.activation(
                rstd_t[:], rstd_t[:], AF.Ln),
                reads=["rstd_ts"], writes=["rstd_ln"])
            p.emit("scalar", lambda e: e.activation(
                rstd_t[:], rstd_t[:], AF.Exp, scale=-0.5),
                reads=["rstd_ln"], writes=["rstd_row"])
            p.emit("tensor", lambda e: e.matmul(
                pb[4][:, 384:385], rstd_t[:], ones1f[0:1, 0:1],
                start=True, stop=True),
                reads=["rstd_row", "ones1f"], writes=pk(4, 384, 512))
            p.emit("vector", lambda e, par=par: e.tensor_copy(
                rstd_c[:, par:par + 1], pb[4][:, 384:385]),
                reads=pk(4, 384, 512), writes=[f"rstd{par}"])

            if par == 1:
                for dq in range(32):
                    nh2, k2 = divmod(dq, 16)
                    slot = dq % 8
                    for kk in range(2):
                        k = 2 * k2 + kk
                        wosl = wbuf[:, slot * 2048 + kk * 1024:
                                    slot * 2048 + kk * 1024 + 1024]
                        wok = wbk(slot * 2048, slot * 2048 + 2048)
                        for cc in range(2):
                            for sg in range(2):
                                bank = cc * 2 + sg
                                p.emit("tensor", lambda e, k=k, w=wosl, cc=cc,
                                       sg=sg, bank=bank: e.matmul(
                                    pb[bank][:], ygv[cc][:, ts(k)],
                                    w[:, ts(sg, 512)],
                                    start=(k == 0), stop=(k == 31)),
                                    reads=wok + [f"yg{cc}"] + kyg[cc],
                                    writes=pk(bank, 0, 512),
                                    inc=(kk == 1 and cc == 1 and sg == 1))
                    if dq + 8 < 32:
                        wo_load(dq + 8)
                    if k2 == 15:
                        for cc in range(2):
                            for sg in range(2):
                                bank = cc * 2 + sg
                                dsl = outst[cc][:, nh2 * 1024 + sg * 512:
                                                nh2 * 1024 + (sg + 1) * 512]
                                p.emit("vector", lambda e, d=dsl, bank=bank,
                                       cc=cc: e.tensor_scalar_mul(
                                        d, pb[bank][:], rstd_c[:, cc:cc + 1]),
                                    reads=pk(bank, 0, 512) + [f"rstd{cc}"] +
                                    kout[cc],
                                    writes=kout[cc] + [f"outst{cc}"])
                for cc in range(2):
                    trow = t0 + (c - 1 + cc) * Q
                    p.emit("scalar", lambda g, tt=tt, cc=cc, c=c: g.dma_start(
                        gtmr, gtm_q[tt, (c - 1 + cc) * Q: (c + cc) * Q, :]),
                        reads=[f"gq{tt}_{c - 1 + cc}"] + kgtmr,
                        writes=kgtmr + ["gtmr"], dma=True)
                    p.emit("scalar", lambda g, trow=trow: g.dma_start(
                        u_rel, u_d[trow: trow + Q, :]),
                        reads=ku_rel, writes=ku_rel + ["u_rel"], dma=True)
                    p.emit("vector", lambda e, cc=cc: e.tensor_tensor(
                        outst[cc], outst[cc], gtmr, ALU.mult),
                        reads=[f"outst{cc}", "gtmr"] + kgtmr + kout[cc],
                        writes=kout[cc] + [f"outst{cc}"])
                    p.emit("vector", lambda e, cc=cc: e.tensor_tensor(
                        outst[cc], outst[cc], u_rel, ALU.add),
                        reads=[f"outst{cc}", "u_rel"] + ku_rel + kout[cc],
                        writes=kout[cc] + [f"outst{cc}"])
                    p.emit("gpsimd", lambda g, trow=trow, cc=cc: g.dma_start(
                        out_d[trow: trow + Q, :], outst[cc]),
                        reads=[f"outst{cc}"] + kout[cc], writes=[f"od{trow}"],
                        dma=True)

    for name, d, ap, rkeys in dbg_items:
        p.emit("gpsimd", lambda g, d=d, ap=ap: g.dma_start(d[:], ap),
               reads=rkeys, writes=[f"dbg{name}"], dma=True)

    tail_waits = {"gpsimd": [("gpsimdd", p.counts["gpsimdd"]),
                             ("syncd", p.counts["syncd"]),
                             ("scalard", p.counts["scalard"])]}
    with nc.Block() as block:
        p.realize(block, sems, tail_waits)
    es.close()
    return nc


def _host_inputs(inputs, core, n_ttiles=NSEQ * NTT_PER_SEQ):
    u = inputs["L0_attn_u"]
    W_in = inputs["L0_attn_W_in"]
    conv_w = inputs["L0_attn_conv_w"]
    conv_b = inputs["L0_attn_conv_b"]
    dt_bias = inputs["L0_attn_dt_bias"]
    A_log = inputs["L0_attn_A_log"]
    Dp = inputs["L0_attn_Dp"]
    norm_w = inputs["L0_attn_norm_w"]
    W_out = inputs["L0_attn_W_out"]
    W0 = inputs["L0_ffn_W0"]
    b0 = inputs["L0_ffn_bias0"]
    W1 = inputs["L0_ffn_W1"]
    b1 = inputs["L0_ffn_bias1"]

    dp2 = np.empty((128, 32), np.float32)
    for g in range(32):
        dp2[:64, g] = Dp[2 * g]
        dp2[64:, g] = Dp[2 * g + 1]
    d = {
        "u": np.ascontiguousarray(u[2 * core: 2 * core + 2].reshape(NSEQ * SEQ, DM)[
            : n_ttiles * TT]),
        "winT": np.ascontiguousarray(W_in.T),
        "w0T": np.ascontiguousarray(W0.T),
        "w1T": np.ascontiguousarray(W1.T),
        "woT": np.ascontiguousarray(W_out.T),
        "cwp": np.ascontiguousarray(
            conv_w.reshape(NFT, 128, 4).transpose(1, 0, 2).reshape(128, NFT * 4)),
        "cbp": np.ascontiguousarray(conv_b.reshape(NFT, 128).T),
        "dtb": np.ascontiguousarray(dt_bias.reshape(NH, 1)),
        "alog": np.ascontiguousarray(A_log.reshape(NH, 1)),
        "dp2": dp2,
        "nw": np.ascontiguousarray(norm_w.reshape(32, 128).T),
        "b0c": np.ascontiguousarray(b0.reshape(64, 128).T),
        "b1c": np.ascontiguousarray(b1.reshape(16, 128).T),
        "identb": np.eye(128, dtype=BFD),
        "identf": np.eye(128, dtype=np.float32),
        "maskrep": np.tile(np.tril(np.full((128, 128), -1e38, np.float32), -1),
                           (1, 4)),
    }
    return d


def kernel(**inputs):
    nc = build()
    in_maps = [_host_inputs(inputs, c) for c in range(8)]
    res = run_bass_kernel_spmd(nc, in_maps, core_ids=list(range(8)))
    out = np.stack([r["out"].reshape(NSEQ, SEQ, DM) for r in res.results])
    return np.ascontiguousarray(out.reshape(16, SEQ, DM)).astype(np.float32)
